# version 17
# baseline (speedup 1.0000x reference)
import numpy as np
import concourse.bass as bass
import concourse.mybir as mybir
from concourse.bass_utils import run_bass_kernel_spmd

F32 = mybir.dt.float32
BF16 = mybir.dt.bfloat16
AF = mybir.ActivationFunctionType
ALU = mybir.AluOpType
AX = mybir.AxisListType

S = 2048
D = 1024
H = 16
DFF = 4096
NEXP = 8
NCH = D // 128
NTI = S // 128
NQC = S // 512
NFF = DFF // 128
PADLEN = 4480
VW = 4224
LN_EPS = 1e-5

ENGS = ("pe", "ve", "se", "gp", "sy")
_DEBUG = False
_PROBE = False


def build_program():
    nc = bass.Bass("TRN2", target_bir_lowering=False, debug=False)

    d_xT = nc.dram_tensor("xT", [D, S], F32, kind="ExternalInput")
    d_wqkv = nc.dram_tensor("wqkv", [128, NCH * 3072], F32, kind="ExternalInput")
    d_wproj = nc.dram_tensor("wproj", [128, NCH * 1024], F32, kind="ExternalInput")
    d_wbqk = nc.dram_tensor("wbqk", [128, 16], F32, kind="ExternalInput")
    d_bv = nc.dram_tensor("bv", [1, 1024], F32, kind="ExternalInput")
    d_pb = nc.dram_tensor("pbias", [128, 8], F32, kind="ExternalInput")
    d_lng = nc.dram_tensor("lng", [128, 8], F32, kind="ExternalInput")
    d_lnb = nc.dram_tensor("lnb", [128, 8], F32, kind="ExternalInput")
    d_rpb = nc.dram_tensor("rpbpad", [H, 128, VW], F32, kind="ExternalInput")
    d_rw = nc.dram_tensor("rw", [128, NCH * 7], F32, kind="ExternalInput")
    d_g = nc.dram_tensor("g_all", [NEXP, NFF, 128, 1024], F32, kind="ExternalInput")
    d_u = nc.dram_tensor("u_all", [NEXP, NFF, 128, 1024], F32, kind="ExternalInput")
    d_dn = nc.dram_tensor("dn_all", [NEXP, 8, 128, 4096], F32, kind="ExternalInput")
    d_out = nc.dram_tensor("out", [S, D], F32, kind="ExternalOutput")

    ident = nc.alloc_sbuf_tensor("ident", [128, 128], F32)
    onescol_bf = nc.alloc_sbuf_tensor("onescol_bf", [128, 8], BF16)
    onesrow_bf = nc.alloc_sbuf_tensor("onesrow_bf", [1, 128], BF16)
    onesrow_f = nc.alloc_sbuf_tensor("onesrow_f", [1, 128], F32)
    ones64_f = nc.alloc_sbuf_tensor("ones64_f", [128, 64], F32)
    wbqk = nc.alloc_sbuf_tensor("wbqk_s", [128, 16], F32)
    wbqs = nc.alloc_sbuf_tensor("wbqs", [128, 8], F32)
    pbias = nc.alloc_sbuf_tensor("pbias_s", [128, 8], F32)
    lng = nc.alloc_sbuf_tensor("lng_s", [128, 8], F32)
    lnb = nc.alloc_sbuf_tensor("lnb_s", [128, 8], F32)
    rw_bf = nc.alloc_sbuf_tensor("rw_bf", [128, NCH * 7], BF16)
    combine = nc.alloc_sbuf_tensor("combine", [128, NTI * 7], F32)
    cbones = nc.alloc_sbuf_tensor("cbones", [128, 1], F32)
    r_er = nc.alloc_sbuf_tensor("r_er", [128, 7], F32)
    r_t1 = nc.alloc_sbuf_tensor("r_t1", [128, 7], F32)
    r_t2 = nc.alloc_sbuf_tensor("r_t2", [128, 7], F32)
    r_m1 = nc.alloc_sbuf_tensor("r_m1", [128, 1], F32)
    r_m2 = nc.alloc_sbuf_tensor("r_m2", [128, 1], F32)
    r_s1 = nc.alloc_sbuf_tensor("r_s1", [128, 1], F32)
    eps_sb = nc.alloc_sbuf_tensor("eps_sb", [1, 1], F32)
    fence_sb = nc.alloc_sbuf_tensor("fence_sb", [1, 16], F32)
    probe_sb = nc.alloc_sbuf_tensor("probe_sb", [128, 64], F32)

    remaining = nc.sbuf_bytes_remaining
    ARENA = remaining - 256
    arena_base = (nc.sbuf_top - ARENA + 63) & ~63
    ARENA = nc.sbuf_top - arena_base

    slot_specs = [
        ("x_sb",   [128, NCH * 2048], F32, 0, 1, 0),
        ("wqkv_sb", [128, NCH * 3072], BF16, 0, 2, 65536),
        ("bv_bf",  [1, 1024], BF16, 0, 2, 114688),
        ("xln1",   [128, NCH * 2048], BF16, 1, 4, 131072),
        ("sc0A",   [128, 2048], BF16, 1, 1, 163840),
        ("sc1A",   [128, 2048], BF16, 1, 1, 167936),
        ("sq0A",   [128, 2048], BF16, 1, 1, 172032),
        ("sq1A",   [128, 2048], BF16, 1, 1, 176128),
        ("row_rstdA", [1, 2048], F32, 1, 1, 180224),
        ("row_mrA", [1, 2048], F32, 1, 1, 188416),
        ("row_tmpA", [1, 2048], F32, 1, 1, 196608),
        ("tmp0A",  [128, 512], F32, 1, 1, 204800),
        ("tmp1A",  [128, 512], F32, 1, 1, 206848),
        ("q_bf",   [128, NCH * 2048], BF16, 2, 4, 0),
        ("k_bf",   [128, NCH * 2048], BF16, 2, 3, 32768),
        ("vaug",   [128, NTI * 1040], BF16, 2, 3, 163840),
        ("vraw",   [128, VW], BF16, 2, 3, 197120),
        ("vexp0",  [128, VW], BF16, 3, 3, 65536),
        ("vexp1",  [128, VW], BF16, 3, 3, 73984),
        ("e_0",    [128, 1024], BF16, 3, 3, 82432),
        ("e_1",    [128, 1024], BF16, 3, 3, 84480),
        ("e2_0",   [128, 1024], BF16, 3, 3, 86528),
        ("e2_1",   [128, 1024], BF16, 3, 3, 88576),
        ("den0",   [128, 512], F32, 3, 3, 90624),
        ("den1",   [128, 512], F32, 3, 3, 92672),
        ("osb0",   [64, 512], F32, 3, 3, 94720),
        ("osb1",   [64, 512], F32, 3, 3, 96768),
        ("to0",    [64, 512], BF16, 3, 3, 98816),
        ("to1",    [64, 512], BF16, 3, 3, 99840),
        ("proj_sb", [128, NCH * 1024], BF16, 4, 4, 32768),
        ("y_sb",   [128, NCH * 2048], F32, 4, 5, 65536),
        ("x2_sb",  [128, NCH * 2048], F32, 5, 7, 0),
        ("sc0B",   [128, 2048], BF16, 5, 6, 131072),
        ("sc1B",   [128, 2048], BF16, 5, 6, 135168),
        ("sq0B",   [128, 2048], BF16, 5, 6, 139264),
        ("sq1B",   [128, 2048], BF16, 5, 6, 143360),
        ("row_rstdB", [1, 2048], F32, 5, 6, 147456),
        ("row_mrB", [1, 2048], F32, 5, 6, 155648),
        ("row_tmpB", [1, 2048], F32, 5, 6, 163840),
        ("tmp0B",  [128, 512], F32, 5, 6, 172032),
        ("tmp1B",  [128, 512], F32, 5, 6, 174080),
        ("xchk0",  [128, 2048], F32, 5, 5, 176128),
        ("xchk1",  [128, 2048], F32, 5, 5, 184320),
        ("xb0",    [128, 512], F32, 5, 5, 192512),
        ("xb1",    [128, 512], F32, 5, 5, 194560),
        ("xln2",   [128, NCH * 2048], BF16, 6, 9, 65536),
        ("acc",    [128, NTI * 1024], F32, 7, 10, 131072),
        ("h_sb",   [128, NFF * 1024], BF16, 9, 9, 0),
        ("gring0", [128, 1024], BF16, 9, 9, 98304),
        ("gring1", [128, 1024], BF16, 9, 9, 100352),
        ("uring0", [128, 1024], BF16, 9, 9, 102400),
        ("uring1", [128, 1024], BF16, 9, 9, 104448),
        ("gsil0",  [128, 512], BF16, 9, 9, 106496),
        ("gsil1",  [128, 512], BF16, 9, 9, 107520),
        ("dring0", [128, 4096], BF16, 9, 9, 108544),
        ("dring1", [128, 4096], BF16, 9, 9, 116736),
        ("qstg",   [128, 4096], BF16, 3, 3, 100864),
    ]

    def nbytes_pp(shape, dtype):
        per = 1
        for sdim in shape[1:]:
            per *= sdim
        b = per * (2 if dtype == BF16 else 4)
        return (b + 63) & ~63

    for i in range(len(slot_specs)):
        n1, sh1, dt1, a0, a1, o1 = slot_specs[i]
        sz1 = nbytes_pp(sh1, dt1)
        assert o1 + sz1 <= ARENA, f"{n1} exceeds arena: {o1 + sz1} > {ARENA}"
        for jj in range(i + 1, len(slot_specs)):
            n2, sh2, dt2, b0, b1, o2 = slot_specs[jj]
            sz2 = nbytes_pp(sh2, dt2)
            if a0 <= b1 and b0 <= a1:
                assert o1 + sz1 <= o2 or o2 + sz2 <= o1, (
                    f"overlap {n1}[{o1},{o1+sz1}) vs {n2}[{o2},{o2+sz2})")
    placed = {name: (off, shape, dtype)
              for (name, shape, dtype, _, _, off) in slot_specs}

    sb = {}
    for name, (off, shape, dtype) in placed.items():
        sb[name] = nc.alloc_sbuf_tensor_at(name, shape, dtype,
                                           offset=arena_base + off)

    x_sb = sb["x_sb"]; wqkv_sb = sb["wqkv_sb"]; bv_bf = sb["bv_bf"]
    xln1 = sb["xln1"]; q_bf = sb["q_bf"]; k_bf = sb["k_bf"]; vaug = sb["vaug"]
    vraw = sb["vraw"]; vexp = [sb["vexp0"], sb["vexp1"]]
    e_sb = [sb["e_0"], sb["e_1"]]; e2_sb = [sb["e2_0"], sb["e2_1"]]
    den = [sb["den0"], sb["den1"]]; osb = [sb["osb0"], sb["osb1"]]
    t_o = [sb["to0"], sb["to1"]]
    proj_sb = sb["proj_sb"]; y_sb = sb["y_sb"]; x2_sb = sb["x2_sb"]
    lnA = dict(sc=[sb["sc0A"], sb["sc1A"]], sq=[sb["sq0A"], sb["sq1A"]],
               rstd=sb["row_rstdA"], mr=sb["row_mrA"], tmpr=sb["row_tmpA"],
               tmp=[sb["tmp0A"], sb["tmp1A"]])
    lnB = dict(sc=[sb["sc0B"], sb["sc1B"]], sq=[sb["sq0B"], sb["sq1B"]],
               rstd=sb["row_rstdB"], mr=sb["row_mrB"], tmpr=sb["row_tmpB"],
               tmp=[sb["tmp0B"], sb["tmp1B"]])
    xchk = [sb["xchk0"], sb["xchk1"]]; xb = [sb["xb0"], sb["xb1"]]
    xln2 = sb["xln2"]; acc = sb["acc"]; h_sb = sb["h_sb"]
    gring = [sb["gring0"], sb["gring1"]]; uring = [sb["uring0"], sb["uring1"]]
    dring = [sb["dring0"], sb["dring1"]]; gsil = [sb["gsil0"], sb["gsil1"]]
    qstg = sb["qstg"]

    DMA_SEMS = ("sx", "sx2", "sconst", "swqkv", "swgp", "sproj", "svraw",
                "sxc0", "sxc1", "sto0", "sto1", "sg0", "sg1", "su0", "su1",
                "sd0", "sd1", "st")
    sems = {n: nc.alloc_semaphore(n) for n in
            (("pe", "ve", "se", "gp") + DMA_SEMS)}
    cnt = {k: 0 for k in sems}
    ops = []
    last_wait = {e: {} for e in ENGS}

    def emit(eng, fn, waits=(), inc=None, n=1):
        ws = []
        for (s, v) in waits:
            if not v or v <= 0:
                continue
            lw = last_wait[eng].get(s, -1)
            if v > lw:
                ws.append((s, v))
                last_wait[eng][s] = v
        if inc is not None:
            cnt[inc] += n
        ops.append((eng, fn, tuple(ws), inc, n if inc else 0))
        return cnt[inc] if inc else None

    def barrier():
        snap = dict(cnt)
        for e in ENGS:
            for s in snap:
                v = snap[s]
                if v > last_wait[e].get(s, -1) and v > 0:
                    ops.append((e, None, ((s, v),), None, 0))
                    last_wait[e][s] = v

    dbg_names = []

    def probe(col, ap):
        if _PROBE:
            ve(lambda E, c=col, a=ap: E.tensor_copy(
                probe_sb[0:a.shape[0], c:c + a.shape[1]], a))

    def dbg_store(name, ap, dtype):
        if not _DEBUG:
            return
        fv = ve(lambda E: E.memset(fence_sb[:], 0.0))
        fs = se(lambda E: E.copy(fence_sb[0:1, 0:1], fence_sb[0:1, 0:1]))
        ops.append(("sy", None, (("ve", fv), ("se", fs)), None, 0))
        dt_ = nc.dram_tensor("dbg_" + name, list(ap.shape), dtype,
                             kind="ExternalOutput")
        emit("sy", lambda E, o=dt_.ap(), i=ap: E.dma_start(out=o, in_=i),
             inc="st", n=16)
        dbg_names.append("dbg_" + name)
        barrier()

    def dma(eng, out_ap, in_ap, sem, waits=()):
        return emit(eng, lambda E, o=out_ap, i=in_ap: E.dma_start(out=o, in_=i),
                    waits=waits, inc=sem, n=16)

    def store(out_ap, in_ap, waits=()):
        return emit("sy", lambda E, o=out_ap, i=in_ap: E.dma_start(out=o, in_=i),
                    waits=waits, inc="st", n=16)

    def mm(out_ap, lhsT, rhs, start, stop, waits=(), inc=False):
        return emit("pe",
                    lambda E, o=out_ap, l=lhsT, r=rhs, s1=start, s2=stop:
                    E.matmul(o, l, r, start=s1, stop=s2),
                    waits=waits, inc=("pe" if inc else None))

    def ve(fn, waits=()):
        return emit("ve", fn, waits=waits, inc="ve")

    def se(fn, waits=()):
        return emit("se", fn, waits=waits, inc="se")

    def gp(fn, waits=()):
        return emit("gp", fn, waits=waits, inc="gp")

    def ve_fence(waits=()):
        return ve(lambda E: E.memset(fence_sb[:], 0.0), waits=waits)

    ps_a = nc.alloc_psum_tensor("ps_a", [128, 2048], F32)
    ps_b = nc.alloc_psum_tensor("ps_b", [128, 2048], F32)

    for d_ in range(NCH):
        dma("sy" if d_ % 2 == 0 else "se",
            x_sb[:, 2048 * d_:2048 * (d_ + 1)],
            d_xT.ap()[128 * d_:128 * (d_ + 1), :],
            "sx" if d_ % 2 == 0 else "sx2")
    ld_x = [("sx", cnt["sx"]), ("sx2", cnt["sx2"])]

    def x_chunk_wait(d_):
        return [("sx" if d_ % 2 == 0 else "sx2", 16 * (d_ // 2 + 1))]
    dma("sy", wbqk[:], d_wbqk.ap(), "sconst")
    dma("sy", pbias[:], d_pb.ap(), "sconst")
    dma("sy", lng[:], d_lng.ap(), "sconst")
    dma("sy", lnb[:], d_lnb.ap(), "sconst")
    ld_consts = ("sconst", cnt["sconst"])
    ld_wbqk = ld_pb = ld_lng = ld_lnb = ld_consts
    for i in range(4):
        dma("gp", wqkv_sb[:, 6144 * i:6144 * (i + 1)],
            d_wqkv.ap()[:, 6144 * i:6144 * (i + 1)], "swqkv")
    w_qkv_m = ("swqkv", cnt["swqkv"])
    dma("gp", bv_bf[:], d_bv.ap(), "swgp")
    dma("gp", rw_bf[:], d_rw.ap(), "swgp")
    ld_wgp = ("swgp", cnt["swgp"])
    ld_bv = ld_rw = ld_wgp

    m_ims = gp(lambda E: E.memset(ident[:], 0.0))
    m_ident = gp(lambda E: E.affine_select(
        out=ident[:], in_=ident[:], compare_op=ALU.not_equal, fill=1.0,
        base=0, pattern=[[-1, 128]], channel_multiplier=1),
        waits=[("gp", m_ims)])
    gp(lambda E: E.memset(onescol_bf[:], 1.0))
    gp(lambda E: E.memset(onesrow_bf[:], 1.0))
    gp(lambda E: E.memset(onesrow_f[:], 1.0))
    gp(lambda E: E.memset(ones64_f[:], 1.0))
    gp(lambda E: E.memset(cbones[:], 1.0))
    m_eps = gp(lambda E: E.memset(eps_sb[:], LN_EPS))
    m_ones = m_eps
    m_wbqs = ve(lambda E: E.tensor_scalar_mul(wbqs[:], wbqk[:, 0:8], 1.0 / 32.0),
                waits=[ld_wbqk])

    def ln_stats(src, src_waits, L, chunk_waits=None):
        sc = L["sc"]; sq = L["sq"]
        row_rstd = L["rstd"]; row_mr = L["mr"]; row_tmp = L["tmpr"]
        sumx = ps_a[0:1, :]
        sumsq = ps_b[0:1, :]
        cast_pe = {}
        sq_pe = {}
        cast_m = {}
        sq_m = {}
        last_x = None
        last_q = None
        for d_ in range(NCH):
            w = (chunk_waits(d_) if chunk_waits else list(src_waits)) + \
                ([("pe", cast_pe[d_ - 2])] if d_ >= 2 else [])
            cast_m[d_] = ("se", se(lambda E, d=d_: E.copy(
                sc[d % 2][:], src[:, 2048 * d:2048 * (d + 1)]), waits=w))
            for qc in range(NQC):
                mmil = mm(sumx[0:1, 512 * qc:512 * (qc + 1)],
                          onescol_bf[:, 0:1],
                          sc[d_ % 2][:, 512 * qc:512 * (qc + 1)],
                          start=(d_ == 0), stop=(d_ == NCH - 1),
                          waits=([cast_m[d_], ("gp", m_ones)]
                                 if qc == 0 else ()),
                          inc=(qc == NQC - 1))
            cast_pe[d_] = mmil
            if d_ == NCH - 1:
                last_x = mmil
        for d_ in range(NCH):
            w = (chunk_waits(d_) if chunk_waits else list(src_waits)) + \
                ([("pe", sq_pe[d_ - 2])] if d_ >= 2 else [])
            sq_m[d_] = ("se", se(lambda E, d=d_: E.activation(
                out=sq[d % 2][:], in_=src[:, 2048 * d:2048 * (d + 1)],
                func=AF.Square), waits=w))
            for qc in range(NQC):
                mmil = mm(sumsq[0:1, 512 * qc:512 * (qc + 1)],
                          onescol_bf[:, 0:1],
                          sq[d_ % 2][:, 512 * qc:512 * (qc + 1)],
                          start=(d_ == 0), stop=(d_ == NCH - 1),
                          waits=([sq_m[d_]] if qc == 0 else ()),
                          inc=(qc == NQC - 1))
            sq_pe[d_] = mmil
            if d_ == NCH - 1:
                last_q = mmil
        m1 = ve(lambda E: E.tensor_scalar_mul(row_mr[:], sumx, 1.0 / D),
                waits=[("pe", last_x)])
        m2 = ve(lambda E: E.tensor_scalar_mul(row_tmp[:], sumsq, 1.0 / D),
                waits=[("pe", last_q), ("ve", m1)])
        m3 = ve(lambda E: E.tensor_mul(row_rstd[:], row_mr[:], row_mr[:]),
                waits=[("ve", m2)])
        m4 = ve(lambda E: E.tensor_sub(row_tmp[:], row_tmp[:], row_rstd[:]),
                waits=[("ve", m3)])
        m5 = se(lambda E: E.activation(out=row_tmp[:], in_=row_tmp[:],
                                       func=AF.Ln, bias=eps_sb[0:1, 0:1]),
                waits=[("ve", m4), ("gp", m_eps)])
        m6 = se(lambda E: E.activation(out=row_rstd[:], in_=row_tmp[:],
                                       func=AF.Exp, scale=-0.5),
                waits=[("se", m5)])
        m7 = ve(lambda E: E.tensor_mul(row_mr[:], row_mr[:], row_rstd[:]),
                waits=[("se", m6)])
        return m7

    def ln_bcast(qc, m_rows, L):
        mm(ps_a[:, 512 * qc:512 * (qc + 1)], onesrow_f[0:1, :],
           L["rstd"][0:1, 512 * qc:512 * (qc + 1)], start=True, stop=True,
           waits=[("ve", m_rows), ("gp", m_ones)])
        return mm(ps_b[:, 512 * qc:512 * (qc + 1)], onesrow_f[0:1, :],
                  L["mr"][0:1, 512 * qc:512 * (qc + 1)], start=True, stop=True,
                  inc=True)

    m_rows1 = ln_stats(x_sb, list(ld_x), lnA, chunk_waits=x_chunk_wait)
    m_xln1 = None
    m_tmp_free = [0, 0]
    for qc in range(NQC):
        mb = ln_bcast(qc, m_rows1, lnA)
        tmp = lnA["tmp"]
        for d_ in range(NCH):
            t = d_ % 2
            ma = ve(lambda E, d=d_, q=qc, tt=t: E.tensor_mul(
                tmp[tt][:],
                x_sb[:, 2048 * d + 512 * q:2048 * d + 512 * (q + 1)],
                ps_a[:, 512 * q:512 * (q + 1)]),
                waits=[("pe", mb), ("ve", m_tmp_free[t])])
            m_xln1 = ve(lambda E, d=d_, q=qc, tt=t: E.tensor_sub(
                xln1[:, 2048 * d + 512 * q:2048 * d + 512 * (q + 1)],
                tmp[tt][:], ps_b[:, 512 * q:512 * (q + 1)]),
                waits=[("ve", ma)])
            m_tmp_free[t] = m_xln1

    barrier()
    dbg_store("xln1", xln1[:], BF16)
    probe(0, xln1[:, 0:4])
    probe(48, lnA["rstd"][0:1, 0:4])
    m_vaug_init = gp(lambda E: E.memset(vaug[:], 1.0))

    psr = [ps_a[:, 512 * r:512 * (r + 1)] for r in range(4)]
    psr_free = [None, None, None, None]

    for j in range(16):
        for qc in range(NQC):
            r = (j * NQC + qc) % 4
            for din in range(NCH):
                w0 = []
                if din == 0:
                    w0 = [w_qkv_m, ("ve", m_xln1)]
                    if psr_free[r]:
                        w0.append(psr_free[r])
                mmil = mm(psr[r],
                          wqkv_sb[:, 3072 * din + 128 * j:3072 * din + 128 * (j + 1)],
                          xln1[:, 2048 * din + 512 * qc:2048 * din + 512 * (qc + 1)],
                          start=(din == 0), stop=(din == NCH - 1),
                          waits=w0, inc=(din == NCH - 1))
            if j < 8:
                dst = q_bf[:, 2048 * j + 512 * qc:2048 * j + 512 * (qc + 1)]
                mse = se(lambda E, ds=dst, jj=j, r_=r: E.activation(
                    out=ds, in_=psr[r_], func=AF.Identity,
                    bias=wbqs[:, jj:jj + 1], scale=1.0 / 32.0),
                    waits=[("pe", mmil), ("ve", m_wbqs)])
            else:
                dst = k_bf[:, 2048 * (j - 8) + 512 * qc:2048 * (j - 8) + 512 * (qc + 1)]
                mse = se(lambda E, ds=dst, jj=j, r_=r: E.activation(
                    out=ds, in_=psr[r_], func=AF.Identity,
                    bias=wbqk[:, jj:jj + 1], scale=1.0),
                    waits=[("pe", mmil)])
            psr_free[r] = ("se", mse)
    m_qk_done = ("se", cnt["se"])
    for ti in range(NTI):
        for vc in range(2):
            r = (ti * 2 + vc) % 4
            for din in range(NCH):
                w0 = []
                if din == 0 and psr_free[r]:
                    w0 = [psr_free[r]]
                mm(psr[r],
                   xln1[:, 2048 * din + 128 * ti:2048 * din + 128 * (ti + 1)],
                   wqkv_sb[:, 3072 * din + 2048 + 512 * vc:
                           3072 * din + 2048 + 512 * (vc + 1)],
                   start=(din == 0), stop=False, waits=w0)
            mmil = mm(psr[r], onesrow_bf[0:1, :],
                      bv_bf[0:1, 512 * vc:512 * (vc + 1)],
                      start=False, stop=True,
                      waits=[ld_bv, ("gp", m_ones)], inc=True)
            dst = bass.AP(vaug, 1040 * ti + 520 * vc,
                          [[NTI * 1040, 128], [65, 8], [1, 64]])
            src3 = bass.AP(ps_a, 512 * r, [[2048, 128], [64, 8], [1, 64]])
            mvv = ve(lambda E, ds=dst, s3=src3: E.tensor_copy(ds, s3),
                     waits=[("pe", mmil), ("gp", m_vaug_init)])
            psr_free[r] = ("ve", mvv)

    barrier()
    probe(4, q_bf[:, 0:4])
    probe(8, k_bf[:, 0:4])
    probe(12, vaug[:, 0:4])
    dbg_store("q", q_bf[:], BF16)
    dbg_store("k", k_bf[:], BF16)
    dbg_store("vaug", vaug[:], BF16)

    m_vexp_se = [0, 0]
    m_vmask = [0, 0]
    m_vcons = [0, 0]
    vstate = {"load_se": 0}

    def v_pipeline_a(h):
        par = h % 2
        mload = dma("gp", vraw[:], d_rpb.ap()[h, :, :],
                    "svraw", waits=[("se", vstate["load_se"])])
        mexp1 = se(lambda E, p=par: E.activation(
            out=vexp[p][:, 0:VW // 2], in_=vraw[:, 0:VW // 2], func=AF.Exp),
            waits=[("svraw", mload), ("ve", m_vcons[par]),
                   ("gp", m_vmask[par])])
        vstate["half"] = (h, mload)

    def v_pipeline_b(h):
        par = h % 2
        _, mload = vstate["half"]
        mexp = se(lambda E, p=par: E.activation(
            out=vexp[p][:, VW // 2:VW], in_=vraw[:, VW // 2:VW], func=AF.Exp),
            waits=[("svraw", mload), ("ve", m_vcons[par]),
                   ("gp", m_vmask[par])])
        vstate["load_se"] = mexp
        m_vexp_se[par] = mexp
        m_vmask[par] = gp(lambda E, p=par: E.affine_select(
            out=vexp[p][:], in_=vexp[p][:], compare_op=ALU.is_ge, fill=0.0,
            base=-2049, pattern=[[1, VW]], channel_multiplier=-1),
            waits=[("se", mexp)])

    v_pipeline_a(0)
    v_pipeline_b(0)

    op_ps = [ps_b[0:65, 0:512], ps_b[0:65, 512:1024]]
    rec_ps = [ps_b[0:64, 1024:1536], ps_b[0:64, 1536:2048]]
    m_exp_set = [0, 0]
    m_e_use = [0, 0]
    m_e2_use = [0, 0]
    m_op_evac = [0, 0]
    m_osb_free = [0, 0]
    m_den_free = [0, 0]
    m_rec_free = [0, 0]
    m_to_free = [0, 0]
    op_rot = 0
    to_rot = 0

    pending_norm = [None]

    def norm_front():
        st_ = pending_norm[0]
        if st_ is None:
            return None
        h, qc, orot, mo = st_
        mden = ve(lambda E, o=orot: E.tensor_copy(den[o][64:65, :],
                                                  op_ps[o][64:65, :]),
                  waits=[("pe", mo), ("pe", m_den_free[orot])])
        mosb = ve(lambda E, o=orot: E.tensor_copy(osb[o][:], op_ps[o][0:64, :]),
                  waits=[("ve", m_osb_free[orot])])
        m_op_evac[orot] = mosb
        mrec = ve(lambda E, o=orot: E.reciprocal(den[o][64:65, :],
                                                 den[o][64:65, :]),
                  waits=[("ve", mden)])
        return (h, qc, orot, mrec, mosb)

    def norm_back(front):
        if front is None:
            return
        pending_norm[0] = None
        h, qc, orot, mrec, mosb = front
        hc = h // 2
        rrot = orot
        mbc = mm(rec_ps[rrot], ones64_f[64:65, 0:64], den[orot][64:65, :],
                 start=True, stop=True,
                 waits=[("ve", mrec), ("ve", m_rec_free[rrot]),
                        ("gp", m_ones)], inc=True)
        m_den_free[orot] = mbc
        if h % 2 == 0:
            dst = q_bf[0:64, 2048 * hc + 512 * qc:2048 * hc + 512 * (qc + 1)]
            mof = ve(lambda E, o=orot, rr=rrot, ds=dst: E.tensor_mul(
                ds, osb[o][:], rec_ps[rr]),
                waits=[("pe", mbc), ("ve", mosb)])
        else:
            tr = to_rot_st[0]
            to_rot_st[0] = 1 - tr
            tosem = "sto%d" % tr
            mof = ve(lambda E, o=orot, rr=rrot, t=tr: E.tensor_mul(
                t_o[t][:], osb[o][:], rec_ps[rr]),
                waits=[("pe", mbc), ("ve", mosb),
                       (tosem, m_to_free[tr])])
            mfence = ve_fence()
            m_to_free[tr] = dma(
                "sy",
                q_bf[64:128, 2048 * hc + 512 * qc:2048 * hc + 512 * (qc + 1)],
                t_o[tr][:], tosem, waits=[("ve", mfence)])
        m_rec_free[rrot] = mof
        m_osb_free[orot] = mof

    to_rot_st = [0]
    m_qstg = gp(lambda E: E.memset(qstg[:], 0.0))
    m_stage = {}
    m_slot_pe = [0] * 8
    unit_last_scores = [0]

    def ensure_stage(h_, qc_):
        if h_ >= H or (h_, qc_) in m_stage:
            return
        slot = (h_ % 2) * 4 + qc_
        hr_ = 64 * (h_ % 2)
        hc_ = h_ // 2
        m_stage[(h_, qc_)] = ve(
            lambda E, s=slot, r=hr_, c=hc_, q=qc_: E.tensor_copy(
                qstg[r:r + 64, 512 * s:512 * (s + 1)],
                q_bf[r:r + 64, 2048 * c + 512 * q:2048 * c + 512 * (q + 1)]),
            waits=[("gp", m_qstg), ("pe", m_slot_pe[slot])])

    for h in range(H):
        par = h % 2
        hc = h // 2
        hr = 64 * (h % 2)
        for qc in range(NQC):
            slot_u = (h % 2) * 4 + qc
            ensure_stage(h, qc)
            if qc == 1 and h + 1 < H:
                v_pipeline_a(h + 1)
            if qc == 2 and h + 1 < H:
                v_pipeline_b(h + 1)
            kmax = 4 * (qc + 1)
            ngrp = kmax // 2
            orot = op_rot
            op_rot = 1 - op_rot
            mo = None
            m_e2g = {}

            def emit_scores(g):
                sset = g % 2
                kj0 = 2 * g
                for t_ in range(2):
                    kj = kj0 + t_
                    w0 = [("se", m_exp_set[sset])] if t_ == 0 else []
                    if g == 0 and t_ == 0:
                        w0.append(("ve", m_stage[(h, qc)]))
                    ms = mm(
                        ps_a[:, 1024 * sset + 512 * t_:1024 * sset + 512 * (t_ + 1)],
                        k_bf[:, 2048 * hc + 128 * kj:2048 * hc + 128 * (kj + 1)],
                        qstg[:, 512 * slot_u:512 * (slot_u + 1)],
                        start=True, stop=True, waits=w0, inc=(t_ == 1))
                    unit_last_scores[0] = ms
                mex = se(lambda E, ss=sset: E.activation(
                    out=e_sb[ss][:], in_=ps_a[:, 1024 * ss:1024 * (ss + 1)],
                    func=AF.Exp),
                    waits=[("pe", ms), ("ve", m_e_use[sset])])
                m_exp_set[sset] = mex
                g0 = 2049 - 128 * kj0 + 512 * qc
                win = bass.AP(vexp[par], g0, [[VW, 128], [-128, 2], [1, 512]])
                ein = bass.AP(e_sb[sset], 0, [[1024, 128], [512, 2], [1, 512]])
                eout = bass.AP(e2_sb[sset], 0, [[1024, 128], [512, 2], [1, 512]])
                me2 = ve(lambda E, eo=eout, ei=ein, w=win: E.tensor_tensor(
                    eo, ei, w, ALU.mult),
                    waits=[("se", mex), ("gp", m_vmask[par]),
                           ("pe", m_e2_use[sset])])
                m_e_use[sset] = me2
                m_vcons[par] = me2
                m_e2g[g] = me2

            def emit_op(g):
                nonlocal mo
                sset = g % 2
                kj0 = 2 * g
                for t_ in range(2):
                    kj = kj0 + t_
                    wv = [("ve", m_e2g[g])]
                    if kj == 0:
                        wv.append(("ve", m_op_evac[orot]))
                    mo = mm(op_ps[orot],
                            bass.AP(vaug, 1040 * kj + 65 * h,
                                    [[NTI * 1040, 128], [1, 65]]),
                            e2_sb[sset][:, 512 * t_:512 * (t_ + 1)],
                            start=(kj == 0), stop=(kj == kmax - 1),
                            waits=wv, inc=(t_ == 1))
                m_e2_use[sset] = mo

            emit_scores(0)
            if qc < NQC - 1:
                ensure_stage(h, qc + 1)
            else:
                ensure_stage(h + 1, 0)
            front = norm_front()
            if ngrp > 1:
                emit_scores(1)
            emit_op(0)
            for g in range(2, ngrp):
                emit_scores(g)
                emit_op(g - 1)
            norm_back(front)
            if ngrp > 1:
                emit_op(ngrp - 1)
            pending_norm[0] = (h, qc, orot, mo)
            m_slot_pe[slot_u] = unit_last_scores[0]
    fr = norm_front()
    norm_back(fr)

    barrier()
    probe(16, q_bf[:, 0:4])
    dbg_store("o", q_bf[:], BF16)
    dbg_store("vexp15", vexp[1][:], BF16)

    for i in range(2):
        dma("gp", proj_sb[:, 4096 * i:4096 * (i + 1)],
            d_wproj.ap()[:, 4096 * i:4096 * (i + 1)], "sproj")
    m_proj_w = ("sproj", cnt["sproj"])
    psr_free = [None, None, None, None]
    m_y = None
    for dd in range(NCH):
        for qc in range(NQC):
            r = (dd * NQC + qc) % 4
            for din in range(NCH):
                w0 = []
                if din == 0:
                    w0 = [m_proj_w]
                    if psr_free[r]:
                        w0.append(psr_free[r])
                mmil = mm(psr[r],
                          proj_sb[:, 1024 * din + 128 * dd:1024 * din + 128 * (dd + 1)],
                          q_bf[:, 2048 * din + 512 * qc:2048 * din + 512 * (qc + 1)],
                          start=(din == 0), stop=(din == NCH - 1),
                          waits=w0, inc=(din == NCH - 1))
            m_y = ve(lambda E, d=dd, q=qc, r_=r: E.scalar_tensor_tensor(
                out=y_sb[:, 2048 * d + 512 * q:2048 * d + 512 * (q + 1)],
                in0=psr[r_], scalar=pbias[:, d:d + 1],
                in1=xln1[:, 2048 * d + 512 * q:2048 * d + 512 * (q + 1)],
                op0=ALU.add, op1=ALU.add),
                waits=[("pe", mmil), ld_pb])
            psr_free[r] = ("ve", m_y)

    barrier()
    probe(20, y_sb[:, 0:4])
    dbg_store("y", y_sb[:], F32)

    m_rows2 = ln_stats(y_sb, [("ve", m_y)], lnB)
    mbs = [ln_bcast(qc, m_rows2, lnB) for qc in range(NQC)]
    tmp = lnB["tmp"]
    ld_xc = [None, None]
    m_xc_cons = [0, 0]
    for d_ in range(2):
        ld_xc[d_] = ("sxc%d" % d_,
                     dma("sy" if d_ == 0 else "se", xchk[d_][:],
                         d_xT.ap()[128 * d_:128 * (d_ + 1), :], "sxc%d" % d_))
    m_x2 = None
    m_tmp_free5 = [0, 0]
    for d_ in range(NCH):
        c = d_ % 2
        for qc in range(NQC):
            t = qc % 2
            ma = ve(lambda E, d=d_, q=qc, tt=t: E.tensor_mul(
                tmp[tt][:], y_sb[:, 2048 * d + 512 * q:2048 * d + 512 * (q + 1)],
                ps_a[:, 512 * q:512 * (q + 1)]),
                waits=[("pe", mbs[qc]), ("ve", m_tmp_free5[t])])
            mxb = ve(lambda E, d=d_, q=qc, tt=t: E.tensor_scalar_add(
                xb[tt][:], xchk[d % 2][:, 512 * q:512 * (q + 1)],
                lnb[:, d:d + 1]),
                waits=[ld_xc[c], ld_lnb, ("ve", ma)])
            mt2 = ve(lambda E, tt=t, q=qc: E.tensor_sub(
                tmp[tt][:], tmp[tt][:], ps_b[:, 512 * q:512 * (q + 1)]),
                waits=[("ve", mxb)])
            m_x2 = ve(lambda E, d=d_, q=qc, tt=t: E.scalar_tensor_tensor(
                out=x2_sb[:, 2048 * d + 512 * q:2048 * d + 512 * (q + 1)],
                in0=tmp[tt][:], scalar=lng[:, d:d + 1], in1=xb[tt][:],
                op0=ALU.mult, op1=ALU.add),
                waits=[ld_lng, ("ve", mt2)])
            m_tmp_free5[t] = m_x2
        m_xc_cons[c] = m_x2
        if d_ + 2 < NCH:
            ld_xc[c] = ("sxc%d" % c,
                        dma("sy" if c == 0 else "se", xchk[c][:],
                            d_xT.ap()[128 * (d_ + 2):128 * (d_ + 3), :],
                            "sxc%d" % c, waits=[("ve", m_xc_cons[c])]))

    barrier()
    probe(24, x2_sb[:, 0:4])
    probe(52, lnB["rstd"][0:1, 0:4])
    dbg_store("x2", x2_sb[:], F32)

    m_rows3 = ln_stats(x2_sb, [("ve", m_x2)], lnB)
    m_xln2 = None
    m_tmp_free6 = [0, 0]
    for qc in range(NQC):
        mb = ln_bcast(qc, m_rows3, lnB)
        for d_ in range(NCH):
            t = d_ % 2
            ma = ve(lambda E, d=d_, q=qc, tt=t: E.tensor_mul(
                tmp[tt][:], x2_sb[:, 2048 * d + 512 * q:2048 * d + 512 * (q + 1)],
                ps_a[:, 512 * q:512 * (q + 1)]),
                waits=[("pe", mb), ("ve", m_tmp_free6[t])])
            m_xln2 = ve(lambda E, d=d_, q=qc, tt=t: E.tensor_sub(
                xln2[:, 2048 * d + 512 * q:2048 * d + 512 * (q + 1)],
                tmp[tt][:], ps_b[:, 512 * q:512 * (q + 1)]),
                waits=[("ve", ma)])
            m_tmp_free6[t] = m_xln2

    barrier()
    probe(28, xln2[:, 0:4])
    dbg_store("xln2", xln2[:], BF16)

    tr_ps = [ps_b[:, 0:128], ps_b[:, 512:640]]
    m_tr_free = [0, 0]
    m_tr = None
    for ti in range(NTI):
        for d_ in range(NCH):
            r = (ti * NCH + d_) % 2
            mt = emit("pe", lambda E, d=d_, t=ti, r_=r: E.transpose(
                tr_ps[r_], x2_sb[:, 2048 * d + 128 * t:2048 * d + 128 * (t + 1)],
                ident[:]),
                waits=[("gp", m_ident), ("ve", m_tr_free[r]),
                       ("ve", m_xln2)], inc="pe")
            m_tr = ve(lambda E, d=d_, t=ti, r_=r: E.tensor_copy(
                acc[:, 1024 * t + 128 * d:1024 * t + 128 * (d + 1)], tr_ps[r_]),
                waits=[("pe", mt)])
            m_tr_free[r] = m_tr

    barrier()
    probe(32, acc[:, 0:4])
    dbg_store("accinit", acc[:], F32)

    lg_ps = [ps_b[0:128, 1024:1536], ps_b[0:128, 1536:2048]]
    m_lg_free = [0, 0]
    m_comb = None
    for ti in range(NTI):
        r = ti % 2
        for din in range(NCH):
            mmil = mm(lg_ps[r][:, 0:7],
                      xln2[:, 2048 * din + 128 * ti:2048 * din + 128 * (ti + 1)],
                      rw_bf[:, 7 * din:7 * (din + 1)],
                      start=(din == 0), stop=(din == NCH - 1),
                      waits=([ld_rw, ("se", m_lg_free[r]), ("ve", m_xln2)]
                             if din == 0 else ()),
                      inc=(din == NCH - 1))
        mneg = ve(lambda E, r_=r: E.reduce_max(r_m1[:], lg_ps[r_][:, 0:7],
                                               axis=AX.X, negate=True),
                  waits=[("pe", mmil), ("ve", m_comb)])
        mer = se(lambda E, r_=r: E.activation(out=r_er[:], in_=lg_ps[r_][:, 0:7],
                                              func=AF.Exp, bias=r_m1[:]),
                 waits=[("ve", mneg)])
        m_lg_free[r] = mer
        ms = ve(lambda E: E.reduce_sum(r_s1[:], r_er[:], axis=AX.X),
                waits=[("se", mer)])
        mrc = ve(lambda E: E.reciprocal(r_s1[:], r_s1[:]), waits=[("ve", ms)])
        mp = ve(lambda E: E.tensor_scalar_mul(r_er[:], r_er[:], r_s1[:]),
                waits=[("ve", mrc)])
        mm1 = ve(lambda E: E.reduce_max(r_m1[:], r_er[:], axis=AX.X),
                 waits=[("ve", mp)])
        meq = ve(lambda E: E.tensor_single_scalar(r_t1[:], r_er[:], r_m1[:],
                                                  ALU.is_ge),
                 waits=[("ve", mm1)])
        mp2 = ve(lambda E: E.scalar_tensor_tensor(
            out=r_t2[:], in0=r_t1[:], scalar=-1e9, in1=r_er[:],
            op0=ALU.mult, op1=ALU.add), waits=[("ve", meq)])
        mm2 = ve(lambda E: E.reduce_max(r_m2[:], r_t2[:], axis=AX.X),
                 waits=[("ve", mp2)])
        mkp = ve(lambda E: E.tensor_single_scalar(r_t1[:], r_er[:], r_m2[:],
                                                  ALU.is_ge),
                 waits=[("ve", mm2)])
        mcb = ve(lambda E: E.tensor_mul(r_t2[:], r_er[:], r_t1[:]),
                 waits=[("ve", mkp)])
        mdn = ve(lambda E: E.tensor_add(r_m1[:], r_m1[:], r_m2[:]),
                 waits=[("ve", mcb)])
        mdn2 = ve(lambda E: E.tensor_scalar_add(r_m1[:], r_m1[:], 1e-9),
                  waits=[("ve", mdn)])
        mrc2 = ve(lambda E: E.reciprocal(r_m1[:], r_m1[:]), waits=[("ve", mdn2)])
        m_comb = ve(lambda E, t=ti: E.tensor_scalar_mul(
            combine[:, 7 * t:7 * (t + 1)], r_t2[:], r_m1[:]),
            waits=[("ve", mrc2)])

    probe(36, combine[:, 0:4])
    dbg_store("combine", combine[:], F32)

    g_ps = [ps_a[:, 0:512], ps_a[:, 512:1024]]
    u_ps = [ps_a[:, 1024:1536], ps_a[:, 1536:2048]]
    da_ps = [ps_a[:, 0:256], ps_a[:, 512:768], ps_a[:, 1024:1280],
             ps_a[:, 1536:1792], ps_b[:, 0:256], ps_b[:, 512:768],
             ps_b[:, 1024:1280], ps_b[:, 1536:1792]]
    m_gfree = [0, 0]
    m_ufree = [0, 0]
    m_gslot = [0, 0]
    m_uslot = [0, 0]
    m_dslot = [0, 0]
    m_dafree = [0] * 8
    m_gsil_free = [0, 0]
    m_lastdown = 0
    grot = 0

    for th in range(2):
        for e in range(NEXP):
            mh = None
            for s_ in range(NFF):
                sl = s_ % 2
                mg_ld = ("sg%d" % sl,
                         dma("gp", gring[sl][:], d_g.ap()[e, s_, :, :],
                             "sg%d" % sl,
                             waits=[("pe", m_gslot[sl])]))
                mu_ld = ("su%d" % sl,
                         dma("gp", uring[sl][:], d_u.ap()[e, s_, :, :],
                             "su%d" % sl,
                             waits=[("pe", m_uslot[sl])]))
                for qc2 in range(2):
                    pr = grot
                    grot = 1 - grot
                    for din in range(NCH):
                        mg = mm(g_ps[pr],
                                gring[sl][:, 128 * din:128 * (din + 1)],
                                xln2[:, 2048 * din + 1024 * th + 512 * qc2:
                                     2048 * din + 1024 * th + 512 * (qc2 + 1)],
                                start=(din == 0), stop=(din == NCH - 1),
                                waits=([mg_ld, ("se", m_gfree[pr]),
                                        ("ve", max(m_dafree)),
                                        ("ve", m_xln2)]
                                       if din == 0 else ()),
                                inc=(din == NCH - 1))
                    for din in range(NCH):
                        mu = mm(u_ps[pr],
                                uring[sl][:, 128 * din:128 * (din + 1)],
                                xln2[:, 2048 * din + 1024 * th + 512 * qc2:
                                     2048 * din + 1024 * th + 512 * (qc2 + 1)],
                                start=(din == 0), stop=(din == NCH - 1),
                                waits=([mu_ld, ("ve", m_ufree[pr]),
                                        ("ve", max(m_dafree)),
                                        ("ve", m_xln2)]
                                       if din == 0 else ()),
                                inc=(din == NCH - 1))
                    if qc2 == 1:
                        m_gslot[sl] = mg
                        m_uslot[sl] = mu
                    msil = se(lambda E, p=pr: E.activation(
                        out=gsil[p][:], in_=g_ps[p], func=AF.Silu),
                        waits=[("pe", mg), ("ve", m_gsil_free[pr])])
                    m_gfree[pr] = msil
                    mh = ve(lambda E, p=pr, DD=s_, q=qc2: E.tensor_mul(
                        h_sb[:, 1024 * DD + 512 * q:1024 * DD + 512 * (q + 1)],
                        gsil[p][:], u_ps[p]),
                        waits=[("se", msil), ("pe", mu), ("pe", m_lastdown)])
                    m_ufree[pr] = mh
                    m_gsil_free[pr] = mh
            for ds in range(4):
                for hb in range(2):
                    sl = (2 * ds + hb) % 2
                    md_ld = ("sd%d" % sl,
                             dma("gp", dring[sl][:],
                                 d_dn.ap()[e, 2 * ds + hb, :, :],
                                 "sd%d" % sl,
                                 waits=[("pe", m_dslot[sl])]))
                    for tl in range(8):
                        ti = 8 * th + tl
                        for Dp in range(16):
                            Dc = 16 * hb + Dp
                            w0 = []
                            if Dp == 0:
                                w0 = [md_ld, ("ve", mh), ("ve", m_tr),
                                      ("se", m_lg_free[0]), ("se", m_lg_free[1]),
                                      ("ve", m_comb)]
                                if hb == 0:
                                    w0.append(("ve", m_dafree[tl]))
                            md = mm(da_ps[tl],
                                    h_sb[:, 1024 * Dc + 128 * tl:1024 * Dc + 128 * (tl + 1)],
                                    dring[sl][:, 256 * Dp:256 * (Dp + 1)],
                                    start=(Dc == 0), stop=(Dc == NFF - 1),
                                    waits=w0, inc=(Dp == 15))
                        if hb == 1:
                            ti_ = 8 * th + tl
                            cb_ap = (cbones[:, 0:1] if e == 0
                                     else combine[:, 7 * ti_ + (e - 1):7 * ti_ + e])
                            m_dafree[tl] = ve(
                                lambda E, p=tl, t=ti_, d2=ds, c=cb_ap: E.scalar_tensor_tensor(
                                    out=acc[:, 1024 * t + 256 * d2:1024 * t + 256 * (d2 + 1)],
                                    in0=da_ps[p], scalar=c,
                                    in1=acc[:, 1024 * t + 256 * d2:1024 * t + 256 * (d2 + 1)],
                                    op0=ALU.mult, op1=ALU.add),
                                waits=[("pe", md), ("ve", m_comb), ("ve", m_tr)])
                    m_dslot[sl] = md
            m_lastdown = md
            if th == 0 and e == 0:
                dbg_store("h_e0", h_sb[:], BF16)
                dbg_store("acc_e0", acc[:], F32)
        if th == 0:
            m_f0 = ve_fence()
            for ti in range(8):
                store(d_out.ap()[128 * ti:128 * (ti + 1), :],
                      acc[:, 1024 * ti:1024 * (ti + 1)], waits=[("ve", m_f0)])

    barrier()

    probe(40, h_sb[:, 0:4])
    probe(44, acc[:, 4:8])
    if _PROBE:
        d_probe = nc.dram_tensor("probe_out", [128, 64], F32, kind="ExternalOutput")
    m_out_fence = ve_fence()
    if _PROBE:
        store(d_probe.ap(), probe_sb[:], waits=[("ve", m_out_fence)])
    for ti in range(8, NTI):
        store(d_out.ap()[128 * ti:128 * (ti + 1), :],
              acc[:, 1024 * ti:1024 * (ti + 1)], waits=[("ve", m_out_fence)])
    ops.append(("sy", None, (("st", cnt["st"]),), None, 0))

    eng_ops = {e: [] for e in ENGS}
    for (eng, fn, ws, incsem, incn) in ops:
        eng_ops[eng].append((fn, ws, incsem, incn))

    def run_engine(E, elist):
        for (fn, ws, incsem, incn) in elist:
            for (s, v) in ws:
                E.wait_ge(sems[s], v)
            if fn is not None:
                inst = fn(E)
                if incsem is not None:
                    inst.then_inc(sems[incsem], incn)

    with nc.Block() as block:
        @block.tensor
        def _(eng):
            run_engine(eng, eng_ops["pe"])

        @block.vector
        def _(eng):
            run_engine(eng, eng_ops["ve"])

        @block.scalar
        def _(eng):
            run_engine(eng, eng_ops["se"])

        @block.gpsimd
        def _(eng):
            run_engine(eng, eng_ops["gp"])

        @block.sync
        def _(eng):
            run_engine(eng, eng_ops["sy"])

    nc._dbg_names = dbg_names
    return nc


_NC_CACHE = None


def _get_program():
    global _NC_CACHE
    if _NC_CACHE is None:
        _NC_CACHE = build_program()
    return _NC_CACHE


def kernel(L0_attn_x, L0_attn_W_w, L0_attn_W_b, L0_attn_proj_w, L0_attn_proj_b,
           L0_attn_ln_g, L0_attn_ln_b, L0_attn_rpb, L0_ffn_shared_up,
           L0_ffn_shared_gate, L0_ffn_shared_down, L0_ffn_routed_up,
           L0_ffn_routed_gate, L0_ffn_routed_down, L0_ffn_router_w):
    x = np.asarray(L0_attn_x, dtype=np.float32)
    W_w = np.asarray(L0_attn_W_w, dtype=np.float32)
    W_b = np.asarray(L0_attn_W_b, dtype=np.float32)
    proj_w = np.asarray(L0_attn_proj_w, dtype=np.float32)
    proj_b = np.asarray(L0_attn_proj_b, dtype=np.float32)
    ln_g = np.asarray(L0_attn_ln_g, dtype=np.float32)
    ln_b = np.asarray(L0_attn_ln_b, dtype=np.float32)
    rpb = np.asarray(L0_attn_rpb, dtype=np.float32)
    s_up = np.asarray(L0_ffn_shared_up, dtype=np.float32)
    s_gate = np.asarray(L0_ffn_shared_gate, dtype=np.float32)
    s_down = np.asarray(L0_ffn_shared_down, dtype=np.float32)
    r_up = np.asarray(L0_ffn_routed_up, dtype=np.float32)
    r_gate = np.asarray(L0_ffn_routed_gate, dtype=np.float32)
    r_down = np.asarray(L0_ffn_routed_down, dtype=np.float32)
    router_w = np.asarray(L0_ffn_router_w, dtype=np.float32)

    wqkvT = np.ascontiguousarray(W_w.T)
    wqkv = np.ascontiguousarray(
        wqkvT.reshape(NCH, 128, 3072).transpose(1, 0, 2).reshape(128, NCH * 3072))
    projT = np.ascontiguousarray(proj_w.T)
    wproj = np.ascontiguousarray(
        projT.reshape(NCH, 128, 1024).transpose(1, 0, 2).reshape(128, NCH * 1024))
    wbqk_h = np.ascontiguousarray(W_b[:2048].reshape(16, 128).T)
    bv_h = np.ascontiguousarray(W_b[2048:].reshape(1, 1024))
    pb_h = np.ascontiguousarray(proj_b.reshape(8, 128).T)
    lng_h = np.ascontiguousarray(ln_g.reshape(8, 128).T)
    lnb_h = np.ascontiguousarray(ln_b.reshape(8, 128).T)
    rpbpad = np.zeros((H, PADLEN), dtype=np.float32)
    rpbpad[:, 127:127 + 4095] = rpb.T
    rpbflip = rpbpad[:, ::-1]
    sw = np.lib.stride_tricks.sliding_window_view(rpbflip, VW, axis=1)
    rpbexp = np.ascontiguousarray(sw[:, 129:257][:, ::-1, :])
    rwT = np.ascontiguousarray(router_w.T)
    rw_h = np.ascontiguousarray(
        rwT.reshape(NCH, 128, 7).transpose(1, 0, 2).reshape(128, NCH * 7))

    gates = np.concatenate([s_gate[None], r_gate], axis=0)
    ups = np.concatenate([s_up[None], r_up], axis=0)
    downs = np.concatenate([s_down[None], r_down], axis=0)

    def pack_gu(w):
        t = w.reshape(NEXP, NFF, 128, NCH, 128)
        t = t.transpose(0, 1, 4, 3, 2)
        return np.ascontiguousarray(t.reshape(NEXP, NFF, 128, NCH * 128))

    def pack_down(w):
        t = w.reshape(NEXP, 4, 256, 2, 16, 128)
        t = t.transpose(0, 1, 3, 5, 4, 2)
        return np.ascontiguousarray(t.reshape(NEXP, 8, 128, 16 * 256))

    g_all = pack_gu(gates)
    u_all = pack_gu(ups)
    dn_all = pack_down(downs)

    nc = _get_program()
    in_maps = []
    for c in range(8):
        in_maps.append({
            "xT": np.ascontiguousarray(x[c].T),
            "wqkv": wqkv, "wproj": wproj, "wbqk": wbqk_h, "bv": bv_h,
            "pbias": pb_h, "lng": lng_h, "lnb": lnb_h, "rpbpad": rpbexp,
            "rw": rw_h, "g_all": g_all, "u_all": u_all, "dn_all": dn_all,
        })
    res = run_bass_kernel_spmd(nc, in_maps, core_ids=list(range(8)))
    return np.stack([res.results[c]["out"] for c in range(8)], axis=0)


# version 20
# speedup vs baseline: 1.0827x; 1.0827x over previous
import numpy as np
import concourse.bass as bass
import concourse.mybir as mybir
from concourse.bass_utils import run_bass_kernel_spmd

F32 = mybir.dt.float32
BF16 = mybir.dt.bfloat16
AF = mybir.ActivationFunctionType
ALU = mybir.AluOpType
AX = mybir.AxisListType

S = 2048
D = 1024
H = 16
DFF = 4096
NEXP = 8
NCH = D // 128
NTI = S // 128
NQC = S // 512
NFF = DFF // 128
PADLEN = 4480
VW = 4224
LN_EPS = 1e-5

ENGS = ("pe", "ve", "se", "gp", "sy")
_DEBUG = False
_PROBE = False


def build_program():
    nc = bass.Bass("TRN2", target_bir_lowering=False, debug=False)

    d_xT = nc.dram_tensor("xT", [D, S], F32, kind="ExternalInput")
    d_wqkv = nc.dram_tensor("wqkv", [128, NCH * 3072], F32, kind="ExternalInput")
    d_wproj = nc.dram_tensor("wproj", [128, NCH * 1024], F32, kind="ExternalInput")
    d_wbqk = nc.dram_tensor("wbqk", [128, 16], F32, kind="ExternalInput")
    d_bv = nc.dram_tensor("bv", [1, 1024], F32, kind="ExternalInput")
    d_pb = nc.dram_tensor("pbias", [128, 8], F32, kind="ExternalInput")
    d_lng = nc.dram_tensor("lng", [128, 8], F32, kind="ExternalInput")
    d_lnb = nc.dram_tensor("lnb", [128, 8], F32, kind="ExternalInput")
    d_rpb = nc.dram_tensor("rpbpad", [H, 128, VW], F32, kind="ExternalInput")
    d_rw = nc.dram_tensor("rw", [128, NCH * 7], F32, kind="ExternalInput")
    d_g = nc.dram_tensor("g_all", [NEXP, NFF, 128, 1024], F32, kind="ExternalInput")
    d_u = nc.dram_tensor("u_all", [NEXP, NFF, 128, 1024], F32, kind="ExternalInput")
    d_dn = nc.dram_tensor("dn_all", [NEXP, 8, 128, 4096], F32, kind="ExternalInput")
    d_out = nc.dram_tensor("out", [S, D], F32, kind="ExternalOutput")

    ident = nc.alloc_sbuf_tensor("ident", [128, 128], F32)
    onescol_bf = nc.alloc_sbuf_tensor("onescol_bf", [128, 8], BF16)
    onesrow_bf = nc.alloc_sbuf_tensor("onesrow_bf", [1, 128], BF16)
    onesrow_f = nc.alloc_sbuf_tensor("onesrow_f", [1, 128], F32)
    ones64_f = nc.alloc_sbuf_tensor("ones64_f", [128, 64], F32)
    wbqk = nc.alloc_sbuf_tensor("wbqk_s", [128, 16], F32)
    wbqs = nc.alloc_sbuf_tensor("wbqs", [128, 8], F32)
    pbias = nc.alloc_sbuf_tensor("pbias_s", [128, 8], F32)
    lng = nc.alloc_sbuf_tensor("lng_s", [128, 8], F32)
    lnb = nc.alloc_sbuf_tensor("lnb_s", [128, 8], F32)
    rw_bf = nc.alloc_sbuf_tensor("rw_bf", [128, NCH * 7], BF16)
    combine = nc.alloc_sbuf_tensor("combine", [128, NTI * 7], F32)
    cbones = nc.alloc_sbuf_tensor("cbones", [128, 1], F32)
    r_er = nc.alloc_sbuf_tensor("r_er", [128, 7], F32)
    r_t1 = nc.alloc_sbuf_tensor("r_t1", [128, 7], F32)
    r_t2 = nc.alloc_sbuf_tensor("r_t2", [128, 7], F32)
    r_m1 = nc.alloc_sbuf_tensor("r_m1", [128, 1], F32)
    r_m2 = nc.alloc_sbuf_tensor("r_m2", [128, 1], F32)
    r_s1 = nc.alloc_sbuf_tensor("r_s1", [128, 1], F32)
    eps_sb = nc.alloc_sbuf_tensor("eps_sb", [1, 1], F32)
    fence_sb = nc.alloc_sbuf_tensor("fence_sb", [1, 16], F32)
    probe_sb = nc.alloc_sbuf_tensor("probe_sb", [128, 64], F32)

    remaining = nc.sbuf_bytes_remaining
    ARENA = remaining - 256
    arena_base = (nc.sbuf_top - ARENA + 63) & ~63
    ARENA = nc.sbuf_top - arena_base

    slot_specs = [
        ("x_sb",   [128, NCH * 2048], F32, 0, 1, 0),
        ("wqkv_sb", [128, NCH * 3072], BF16, 0, 2, 65536),
        ("bv_bf",  [1, 1024], BF16, 0, 2, 114688),
        ("xln1",   [128, NCH * 2048], BF16, 1, 4, 131072),
        ("sc0A",   [128, 2048], BF16, 1, 1, 163840),
        ("sc1A",   [128, 2048], BF16, 1, 1, 167936),
        ("sq0A",   [128, 2048], BF16, 1, 1, 172032),
        ("sq1A",   [128, 2048], BF16, 1, 1, 176128),
        ("row_rstdA", [1, 2048], F32, 1, 1, 180224),
        ("row_mrA", [1, 2048], F32, 1, 1, 188416),
        ("row_tmpA", [1, 2048], F32, 1, 1, 196608),
        ("tmp0A",  [128, 512], F32, 1, 1, 204800),
        ("tmp1A",  [128, 512], F32, 1, 1, 206848),
        ("q_bf",   [128, NCH * 2048], BF16, 2, 4, 0),
        ("k_bf",   [128, NCH * 2048], BF16, 2, 3, 32768),
        ("vaug",   [128, NTI * 1040], BF16, 2, 3, 163840),
        ("vraw",   [128, VW], BF16, 2, 3, 197120),
        ("vexp0",  [128, VW], BF16, 3, 3, 65536),
        ("vexp1",  [128, VW], BF16, 3, 3, 73984),
        ("e_0",    [128, 1024], BF16, 3, 3, 82432),
        ("e_1",    [128, 1024], BF16, 3, 3, 84480),
        ("e2_0",   [128, 1024], BF16, 3, 3, 86528),
        ("e2_1",   [128, 1024], BF16, 3, 3, 88576),
        ("den0",   [128, 512], F32, 3, 3, 90624),
        ("den1",   [128, 512], F32, 3, 3, 92672),
        ("osb0",   [64, 512], F32, 3, 3, 94720),
        ("osb1",   [64, 512], F32, 3, 3, 96768),
        ("to0",    [64, 512], BF16, 3, 3, 98816),
        ("to1",    [64, 512], BF16, 3, 3, 99840),
        ("proj_sb", [128, NCH * 1024], BF16, 4, 4, 32768),
        ("y_sb",   [128, NCH * 2048], F32, 4, 5, 65536),
        ("x2_sb",  [128, NCH * 2048], F32, 5, 7, 0),
        ("sc0B",   [128, 2048], BF16, 5, 6, 131072),
        ("sc1B",   [128, 2048], BF16, 5, 6, 135168),
        ("sq0B",   [128, 2048], BF16, 5, 6, 139264),
        ("sq1B",   [128, 2048], BF16, 5, 6, 143360),
        ("row_rstdB", [1, 2048], F32, 5, 6, 147456),
        ("row_mrB", [1, 2048], F32, 5, 6, 155648),
        ("row_tmpB", [1, 2048], F32, 5, 6, 163840),
        ("tmp0B",  [128, 512], F32, 5, 6, 172032),
        ("tmp1B",  [128, 512], F32, 5, 6, 174080),
        ("xchk0",  [128, 2048], F32, 5, 5, 176128),
        ("xchk1",  [128, 2048], F32, 5, 5, 184320),
        ("xb0",    [128, 512], F32, 5, 5, 192512),
        ("xb1",    [128, 512], F32, 5, 5, 194560),
        ("xln2",   [128, NCH * 2048], BF16, 6, 9, 65536),
        ("acc",    [128, NTI * 1024], F32, 7, 10, 131072),
        ("h_sb",   [128, NFF * 1024], BF16, 9, 9, 0),
        ("gring0", [128, 1024], BF16, 9, 9, 98304),
        ("gring1", [128, 1024], BF16, 9, 9, 100352),
        ("uring0", [128, 1024], BF16, 9, 9, 102400),
        ("uring1", [128, 1024], BF16, 9, 9, 104448),
        ("gsil0",  [128, 512], BF16, 9, 9, 106496),
        ("gsil1",  [128, 512], BF16, 9, 9, 107520),
        ("dring0", [128, 4096], BF16, 9, 9, 108544),
        ("dring1", [128, 4096], BF16, 9, 9, 116736),
        ("qstg",   [128, 4096], BF16, 3, 3, 100864),
    ]

    def nbytes_pp(shape, dtype):
        per = 1
        for sdim in shape[1:]:
            per *= sdim
        b = per * (2 if dtype == BF16 else 4)
        return (b + 63) & ~63

    for i in range(len(slot_specs)):
        n1, sh1, dt1, a0, a1, o1 = slot_specs[i]
        sz1 = nbytes_pp(sh1, dt1)
        assert o1 + sz1 <= ARENA, f"{n1} exceeds arena: {o1 + sz1} > {ARENA}"
        for jj in range(i + 1, len(slot_specs)):
            n2, sh2, dt2, b0, b1, o2 = slot_specs[jj]
            sz2 = nbytes_pp(sh2, dt2)
            if a0 <= b1 and b0 <= a1:
                assert o1 + sz1 <= o2 or o2 + sz2 <= o1, (
                    f"overlap {n1}[{o1},{o1+sz1}) vs {n2}[{o2},{o2+sz2})")
    placed = {name: (off, shape, dtype)
              for (name, shape, dtype, _, _, off) in slot_specs}

    sb = {}
    for name, (off, shape, dtype) in placed.items():
        sb[name] = nc.alloc_sbuf_tensor_at(name, shape, dtype,
                                           offset=arena_base + off)

    x_sb = sb["x_sb"]; wqkv_sb = sb["wqkv_sb"]; bv_bf = sb["bv_bf"]
    xln1 = sb["xln1"]; q_bf = sb["q_bf"]; k_bf = sb["k_bf"]; vaug = sb["vaug"]
    vraw = sb["vraw"]; vexp = [sb["vexp0"], sb["vexp1"]]
    e_sb = [sb["e_0"], sb["e_1"]]; e2_sb = [sb["e2_0"], sb["e2_1"]]
    den = [sb["den0"], sb["den1"]]; osb = [sb["osb0"], sb["osb1"]]
    t_o = [sb["to0"], sb["to1"]]
    proj_sb = sb["proj_sb"]; y_sb = sb["y_sb"]; x2_sb = sb["x2_sb"]
    lnA = dict(sc=[sb["sc0A"], sb["sc1A"]], sq=[sb["sq0A"], sb["sq1A"]],
               rstd=sb["row_rstdA"], mr=sb["row_mrA"], tmpr=sb["row_tmpA"],
               tmp=[sb["tmp0A"], sb["tmp1A"]])
    lnB = dict(sc=[sb["sc0B"], sb["sc1B"]], sq=[sb["sq0B"], sb["sq1B"]],
               rstd=sb["row_rstdB"], mr=sb["row_mrB"], tmpr=sb["row_tmpB"],
               tmp=[sb["tmp0B"], sb["tmp1B"]])
    xchk = [sb["xchk0"], sb["xchk1"]]; xb = [sb["xb0"], sb["xb1"]]
    xln2 = sb["xln2"]; acc = sb["acc"]; h_sb = sb["h_sb"]
    gring = [sb["gring0"], sb["gring1"]]; uring = [sb["uring0"], sb["uring1"]]
    dring = [sb["dring0"], sb["dring1"]]; gsil = [sb["gsil0"], sb["gsil1"]]
    qstg = sb["qstg"]

    DMA_SEMS = ("sx", "sx2", "sconst", "swqkv", "swgp", "sproj", "svraw",
                "sxc0", "sxc1", "sto0", "sto1", "sg0", "sg1", "su0", "su1",
                "sd0", "sd1", "st")
    sems = {n: nc.alloc_semaphore(n) for n in
            (("pe", "ve", "se", "gp") + DMA_SEMS)}
    cnt = {k: 0 for k in sems}
    ops = []
    last_wait = {e: {} for e in ENGS}

    def emit(eng, fn, waits=(), inc=None, n=1):
        ws = []
        for (s, v) in waits:
            if not v or v <= 0:
                continue
            lw = last_wait[eng].get(s, -1)
            if v > lw:
                ws.append((s, v))
                last_wait[eng][s] = v
        if inc is not None:
            cnt[inc] += n
        ops.append((eng, fn, tuple(ws), inc, n if inc else 0))
        return cnt[inc] if inc else None

    def barrier():
        snap = dict(cnt)
        for e in ENGS:
            for s in snap:
                v = snap[s]
                if v > last_wait[e].get(s, -1) and v > 0:
                    ops.append((e, None, ((s, v),), None, 0))
                    last_wait[e][s] = v

    dbg_names = []

    def probe(col, ap):
        if _PROBE:
            ve(lambda E, c=col, a=ap: E.tensor_copy(
                probe_sb[0:a.shape[0], c:c + a.shape[1]], a))

    def dbg_store(name, ap, dtype):
        if not _DEBUG:
            return
        fv = ve(lambda E: E.memset(fence_sb[:], 0.0))
        fs = se(lambda E: E.copy(fence_sb[0:1, 0:1], fence_sb[0:1, 0:1]))
        ops.append(("sy", None, (("ve", fv), ("se", fs)), None, 0))
        dt_ = nc.dram_tensor("dbg_" + name, list(ap.shape), dtype,
                             kind="ExternalOutput")
        emit("sy", lambda E, o=dt_.ap(), i=ap: E.dma_start(out=o, in_=i),
             inc="st", n=16)
        dbg_names.append("dbg_" + name)
        barrier()

    def dma(eng, out_ap, in_ap, sem, waits=()):
        return emit(eng, lambda E, o=out_ap, i=in_ap: E.dma_start(out=o, in_=i),
                    waits=waits, inc=sem, n=16)

    def store(out_ap, in_ap, waits=()):
        return emit("sy", lambda E, o=out_ap, i=in_ap: E.dma_start(out=o, in_=i),
                    waits=waits, inc="st", n=16)

    def mm(out_ap, lhsT, rhs, start, stop, waits=(), inc=False):
        return emit("pe",
                    lambda E, o=out_ap, l=lhsT, r=rhs, s1=start, s2=stop:
                    E.matmul(o, l, r, start=s1, stop=s2),
                    waits=waits, inc=("pe" if inc else None))

    def ve(fn, waits=()):
        return emit("ve", fn, waits=waits, inc="ve")

    def se(fn, waits=()):
        return emit("se", fn, waits=waits, inc="se")

    def gp(fn, waits=()):
        return emit("gp", fn, waits=waits, inc="gp")

    def ve_fence(waits=()):
        return ve(lambda E: E.memset(fence_sb[:], 0.0), waits=waits)

    ps_a = nc.alloc_psum_tensor("ps_a", [128, 2048], F32)
    ps_b = nc.alloc_psum_tensor("ps_b", [128, 2048], F32)

    for d_ in range(NCH):
        dma("sy" if d_ % 2 == 0 else "se",
            x_sb[:, 2048 * d_:2048 * (d_ + 1)],
            d_xT.ap()[128 * d_:128 * (d_ + 1), :],
            "sx" if d_ % 2 == 0 else "sx2")
    ld_x = [("sx", cnt["sx"]), ("sx2", cnt["sx2"])]

    def x_chunk_wait(d_):
        return [("sx" if d_ % 2 == 0 else "sx2", 16 * (d_ // 2 + 1))]
    dma("sy", wbqk[:], d_wbqk.ap(), "sconst")
    dma("sy", pbias[:], d_pb.ap(), "sconst")
    dma("sy", lng[:], d_lng.ap(), "sconst")
    dma("sy", lnb[:], d_lnb.ap(), "sconst")
    ld_consts = ("sconst", cnt["sconst"])
    ld_wbqk = ld_pb = ld_lng = ld_lnb = ld_consts
    for i in range(4):
        dma("gp", wqkv_sb[:, 6144 * i:6144 * (i + 1)],
            d_wqkv.ap()[:, 6144 * i:6144 * (i + 1)], "swqkv")
    w_qkv_m = ("swqkv", cnt["swqkv"])
    dma("gp", bv_bf[:], d_bv.ap(), "swgp")
    dma("gp", rw_bf[:], d_rw.ap(), "swgp")
    ld_wgp = ("swgp", cnt["swgp"])
    ld_bv = ld_rw = ld_wgp

    m_ims = gp(lambda E: E.memset(ident[:], 0.0))
    m_ident = gp(lambda E: E.affine_select(
        out=ident[:], in_=ident[:], compare_op=ALU.not_equal, fill=1.0,
        base=0, pattern=[[-1, 128]], channel_multiplier=1),
        waits=[("gp", m_ims)])
    gp(lambda E: E.memset(onescol_bf[:], 1.0))
    gp(lambda E: E.memset(onesrow_bf[:], 1.0))
    gp(lambda E: E.memset(onesrow_f[:], 1.0))
    gp(lambda E: E.memset(ones64_f[:], 1.0))
    gp(lambda E: E.memset(cbones[:], 1.0))
    m_eps = gp(lambda E: E.memset(eps_sb[:], LN_EPS))
    m_ones = m_eps
    m_wbqs = ve(lambda E: E.tensor_scalar_mul(wbqs[:], wbqk[:, 0:8], 1.0 / 32.0),
                waits=[ld_wbqk])

    def ln_stats(src, src_waits, L, chunk_waits=None):
        sc = L["sc"]; sq = L["sq"]
        row_rstd = L["rstd"]; row_mr = L["mr"]; row_tmp = L["tmpr"]
        sumx = ps_a[0:1, :]
        sumsq = ps_b[0:1, :]
        cast_pe = {}
        sq_pe = {}
        cast_m = {}
        sq_m = {}
        last_x = None
        last_q = None
        for d_ in range(NCH):
            w = (chunk_waits(d_) if chunk_waits else list(src_waits)) + \
                ([("pe", cast_pe[d_ - 2])] if d_ >= 2 else [])
            cast_m[d_] = ("se", se(lambda E, d=d_: E.copy(
                sc[d % 2][:], src[:, 2048 * d:2048 * (d + 1)]), waits=w))
            for qc in range(NQC):
                mmil = mm(sumx[0:1, 512 * qc:512 * (qc + 1)],
                          onescol_bf[:, 0:1],
                          sc[d_ % 2][:, 512 * qc:512 * (qc + 1)],
                          start=(d_ == 0), stop=(d_ == NCH - 1),
                          waits=([cast_m[d_], ("gp", m_ones)]
                                 if qc == 0 else ()),
                          inc=(qc == NQC - 1))
            cast_pe[d_] = mmil
            if d_ == NCH - 1:
                last_x = mmil
        for d_ in range(NCH):
            w = (chunk_waits(d_) if chunk_waits else list(src_waits)) + \
                ([("pe", sq_pe[d_ - 2])] if d_ >= 2 else [])
            sq_m[d_] = ("se", se(lambda E, d=d_: E.activation(
                out=sq[d % 2][:], in_=src[:, 2048 * d:2048 * (d + 1)],
                func=AF.Square), waits=w))
            for qc in range(NQC):
                mmil = mm(sumsq[0:1, 512 * qc:512 * (qc + 1)],
                          onescol_bf[:, 0:1],
                          sq[d_ % 2][:, 512 * qc:512 * (qc + 1)],
                          start=(d_ == 0), stop=(d_ == NCH - 1),
                          waits=([sq_m[d_]] if qc == 0 else ()),
                          inc=(qc == NQC - 1))
            sq_pe[d_] = mmil
            if d_ == NCH - 1:
                last_q = mmil
        m1 = ve(lambda E: E.tensor_scalar_mul(row_mr[:], sumx, 1.0 / D),
                waits=[("pe", last_x)])
        m2 = ve(lambda E: E.tensor_scalar_mul(row_tmp[:], sumsq, 1.0 / D),
                waits=[("pe", last_q), ("ve", m1)])
        m3 = ve(lambda E: E.tensor_mul(row_rstd[:], row_mr[:], row_mr[:]),
                waits=[("ve", m2)])
        m4 = ve(lambda E: E.tensor_sub(row_tmp[:], row_tmp[:], row_rstd[:]),
                waits=[("ve", m3)])
        m5 = se(lambda E: E.activation(out=row_tmp[:], in_=row_tmp[:],
                                       func=AF.Ln, bias=eps_sb[0:1, 0:1]),
                waits=[("ve", m4), ("gp", m_eps)])
        m6 = se(lambda E: E.activation(out=row_rstd[:], in_=row_tmp[:],
                                       func=AF.Exp, scale=-0.5),
                waits=[("se", m5)])
        m7 = ve(lambda E: E.tensor_mul(row_mr[:], row_mr[:], row_rstd[:]),
                waits=[("se", m6)])
        return m7

    def ln_bcast(qc, m_rows, L):
        mm(ps_a[:, 512 * qc:512 * (qc + 1)], onesrow_f[0:1, :],
           L["rstd"][0:1, 512 * qc:512 * (qc + 1)], start=True, stop=True,
           waits=[("ve", m_rows), ("gp", m_ones)])
        return mm(ps_b[:, 512 * qc:512 * (qc + 1)], onesrow_f[0:1, :],
                  L["mr"][0:1, 512 * qc:512 * (qc + 1)], start=True, stop=True,
                  inc=True)

    m_rows1 = ln_stats(x_sb, list(ld_x), lnA, chunk_waits=x_chunk_wait)
    m_xln1 = None
    m_tmp_free = [0, 0]
    for qc in range(NQC):
        mb = ln_bcast(qc, m_rows1, lnA)
        tmp = lnA["tmp"]
        for d_ in range(NCH):
            t = d_ % 2
            ma = ve(lambda E, d=d_, q=qc, tt=t: E.tensor_mul(
                tmp[tt][:],
                x_sb[:, 2048 * d + 512 * q:2048 * d + 512 * (q + 1)],
                ps_a[:, 512 * q:512 * (q + 1)]),
                waits=[("pe", mb), ("ve", m_tmp_free[t])])
            m_xln1 = ve(lambda E, d=d_, q=qc, tt=t: E.tensor_sub(
                xln1[:, 2048 * d + 512 * q:2048 * d + 512 * (q + 1)],
                tmp[tt][:], ps_b[:, 512 * q:512 * (q + 1)]),
                waits=[("ve", ma)])
            m_tmp_free[t] = m_xln1

    barrier()
    dbg_store("xln1", xln1[:], BF16)
    probe(0, xln1[:, 0:4])
    probe(48, lnA["rstd"][0:1, 0:4])
    m_vaug_init = gp(lambda E: E.memset(vaug[:], 1.0))

    psr = [ps_a[:, 512 * r:512 * (r + 1)] for r in range(4)]
    psr_free = [None, None, None, None]

    for j in range(16):
        for qc in range(NQC):
            r = (j * NQC + qc) % 4
            for din in range(NCH):
                w0 = []
                if din == 0:
                    w0 = [w_qkv_m, ("ve", m_xln1)]
                    if psr_free[r]:
                        w0.append(psr_free[r])
                mmil = mm(psr[r],
                          wqkv_sb[:, 3072 * din + 128 * j:3072 * din + 128 * (j + 1)],
                          xln1[:, 2048 * din + 512 * qc:2048 * din + 512 * (qc + 1)],
                          start=(din == 0), stop=(din == NCH - 1),
                          waits=w0, inc=(din == NCH - 1))
            if j < 8:
                dst = q_bf[:, 2048 * j + 512 * qc:2048 * j + 512 * (qc + 1)]
                mse = se(lambda E, ds=dst, jj=j, r_=r: E.activation(
                    out=ds, in_=psr[r_], func=AF.Identity,
                    bias=wbqs[:, jj:jj + 1], scale=1.0 / 32.0),
                    waits=[("pe", mmil), ("ve", m_wbqs)])
            else:
                dst = k_bf[:, 2048 * (j - 8) + 512 * qc:2048 * (j - 8) + 512 * (qc + 1)]
                mse = se(lambda E, ds=dst, jj=j, r_=r: E.activation(
                    out=ds, in_=psr[r_], func=AF.Identity,
                    bias=wbqk[:, jj:jj + 1], scale=1.0),
                    waits=[("pe", mmil)])
            psr_free[r] = ("se", mse)
    m_qk_done = ("se", cnt["se"])
    for ti in range(NTI):
        for vc in range(2):
            r = (ti * 2 + vc) % 4
            for din in range(NCH):
                w0 = []
                if din == 0 and psr_free[r]:
                    w0 = [psr_free[r]]
                mm(psr[r],
                   xln1[:, 2048 * din + 128 * ti:2048 * din + 128 * (ti + 1)],
                   wqkv_sb[:, 3072 * din + 2048 + 512 * vc:
                           3072 * din + 2048 + 512 * (vc + 1)],
                   start=(din == 0), stop=False, waits=w0)
            mmil = mm(psr[r], onesrow_bf[0:1, :],
                      bv_bf[0:1, 512 * vc:512 * (vc + 1)],
                      start=False, stop=True,
                      waits=[ld_bv, ("gp", m_ones)], inc=True)
            dst = bass.AP(vaug, 1040 * ti + 520 * vc,
                          [[NTI * 1040, 128], [65, 8], [1, 64]])
            src3 = bass.AP(ps_a, 512 * r, [[2048, 128], [64, 8], [1, 64]])
            mvv = ve(lambda E, ds=dst, s3=src3: E.tensor_copy(ds, s3),
                     waits=[("pe", mmil), ("gp", m_vaug_init)])
            psr_free[r] = ("ve", mvv)

    barrier()
    probe(4, q_bf[:, 0:4])
    probe(8, k_bf[:, 0:4])
    probe(12, vaug[:, 0:4])
    dbg_store("q", q_bf[:], BF16)
    dbg_store("k", k_bf[:], BF16)
    dbg_store("vaug", vaug[:], BF16)

    m_vexp_se = [0, 0]
    m_vmask = [0, 0]
    m_vcons = [0, 0]
    vstate = {"load_se": 0}

    def v_pipeline_a(h):
        par = h % 2
        mload = dma("gp", vraw[:], d_rpb.ap()[h, :, :],
                    "svraw", waits=[("se", vstate["load_se"])])
        mexp1 = se(lambda E, p=par: E.activation(
            out=vexp[p][:, 0:VW // 2], in_=vraw[:, 0:VW // 2], func=AF.Exp),
            waits=[("svraw", mload), ("ve", m_vcons[par]),
                   ("gp", m_vmask[par])])
        vstate["half"] = (h, mload)

    def v_pipeline_b(h):
        par = h % 2
        _, mload = vstate["half"]
        mexp = se(lambda E, p=par: E.activation(
            out=vexp[p][:, VW // 2:VW], in_=vraw[:, VW // 2:VW], func=AF.Exp),
            waits=[("svraw", mload), ("ve", m_vcons[par]),
                   ("gp", m_vmask[par])])
        vstate["load_se"] = mexp
        m_vexp_se[par] = mexp
        m_vmask[par] = gp(lambda E, p=par: E.affine_select(
            out=vexp[p][:], in_=vexp[p][:], compare_op=ALU.is_ge, fill=0.0,
            base=-2049, pattern=[[1, VW]], channel_multiplier=-1),
            waits=[("se", mexp)])

    v_pipeline_a(0)
    v_pipeline_b(0)

    op_ps = [ps_b[0:65, 0:512], ps_b[0:65, 512:1024]]
    rec_ps = [ps_b[0:64, 1024:1536], ps_b[0:64, 1536:2048]]
    m_exp_set = [0, 0]
    m_e_use = [0, 0]
    m_e2_use = [0, 0]
    m_op_evac = [0, 0]
    m_osb_free = [0, 0]
    m_den_free = [0, 0]
    m_rec_free = [0, 0]
    m_to_free = [0, 0]
    m_outv = [0, 0, 0, 0]
    m_outd = [dict() for _ in range(4)]
    op_rot = 0
    to_rot = 0

    pending_norm = [None]

    def norm_front():
        st_ = pending_norm[0]
        if st_ is None:
            return None
        h, qc, orot, mo = st_
        mden = ve(lambda E, o=orot: E.tensor_copy(den[o][64:65, :],
                                                  op_ps[o][64:65, :]),
                  waits=[("pe", mo), ("pe", m_den_free[orot])])
        mosb = ve(lambda E, o=orot: E.tensor_copy(osb[o][:], op_ps[o][0:64, :]),
                  waits=[("ve", m_osb_free[orot])])
        m_op_evac[orot] = mosb
        mln = se(lambda E, o=orot: E.activation(
            out=den[o][64:65, :], in_=den[o][64:65, :], func=AF.Ln),
            waits=[("ve", mden)])
        mrec = se(lambda E, o=orot: E.activation(
            out=den[o][64:65, :], in_=den[o][64:65, :], func=AF.Exp,
            scale=-1.0), waits=[("se", mln)])
        return (h, qc, orot, mrec, mosb)

    def norm_back(front):
        if front is None:
            return
        pending_norm[0] = None
        h, qc, orot, mrec, mosb = front
        hc = h // 2
        rrot = orot
        mbc = mm(rec_ps[rrot], ones64_f[64:65, 0:64], den[orot][64:65, :],
                 start=True, stop=True,
                 waits=[("se", mrec), ("ve", m_rec_free[rrot]),
                        ("gp", m_ones)], inc=True)
        m_den_free[orot] = mbc
        if h % 2 == 0:
            dst = q_bf[0:64, 2048 * hc + 512 * qc:2048 * hc + 512 * (qc + 1)]
            mof = ve(lambda E, o=orot, rr=rrot, ds=dst: E.tensor_mul(
                ds, osb[o][:], rec_ps[rr]),
                waits=[("pe", mbc), ("ve", mosb)])
            m_outv[qc] = mof
        else:
            tr = to_rot_st[0]
            to_rot_st[0] = 1 - tr
            tosem = "sto%d" % tr
            mof = ve(lambda E, o=orot, rr=rrot, t=tr: E.tensor_mul(
                t_o[t][:], osb[o][:], rec_ps[rr]),
                waits=[("pe", mbc), ("ve", mosb),
                       (tosem, m_to_free[tr])])
            mfence = ve_fence()
            m_to_free[tr] = dma(
                "sy",
                q_bf[64:128, 2048 * hc + 512 * qc:2048 * hc + 512 * (qc + 1)],
                t_o[tr][:], tosem, waits=[("ve", mfence)])
            m_outd[qc][tosem] = m_to_free[tr]
        m_rec_free[rrot] = mof
        m_osb_free[orot] = mof

    to_rot_st = [0]
    m_qstg = gp(lambda E: E.memset(qstg[:], 0.0))
    m_stage = {}
    m_slot_pe = [0] * 8
    unit_last_scores = [0]

    def ensure_stage(h_, qc_):
        if h_ >= H or (h_, qc_) in m_stage:
            return
        slot = (h_ % 2) * 4 + qc_
        hr_ = 64 * (h_ % 2)
        hc_ = h_ // 2
        m_stage[(h_, qc_)] = ve(
            lambda E, s=slot, r=hr_, c=hc_, q=qc_: E.tensor_copy(
                qstg[r:r + 64, 512 * s:512 * (s + 1)],
                q_bf[r:r + 64, 2048 * c + 512 * q:2048 * c + 512 * (q + 1)]),
            waits=[("gp", m_qstg), ("pe", m_slot_pe[slot])])

    for h in range(H):
        par = h % 2
        hc = h // 2
        hr = 64 * (h % 2)
        for qc in range(NQC):
            slot_u = (h % 2) * 4 + qc
            ensure_stage(h, qc)
            if qc == 1 and h + 1 < H:
                v_pipeline_a(h + 1)
            if qc == 2 and h + 1 < H:
                v_pipeline_b(h + 1)
            kmax = 4 * (qc + 1)
            ngrp = kmax // 2
            orot = op_rot
            op_rot = 1 - op_rot
            mo = None
            m_e2g = {}

            def emit_scores(g):
                sset = g % 2
                kj0 = 2 * g
                for t_ in range(2):
                    kj = kj0 + t_
                    w0 = [("se", m_exp_set[sset])] if t_ == 0 else []
                    if g == 0 and t_ == 0:
                        w0.append(("ve", m_stage[(h, qc)]))
                    ms = mm(
                        ps_a[:, 1024 * sset + 512 * t_:1024 * sset + 512 * (t_ + 1)],
                        k_bf[:, 2048 * hc + 128 * kj:2048 * hc + 128 * (kj + 1)],
                        qstg[:, 512 * slot_u:512 * (slot_u + 1)],
                        start=True, stop=True, waits=w0, inc=(t_ == 1))
                    unit_last_scores[0] = ms
                mex = se(lambda E, ss=sset: E.activation(
                    out=e_sb[ss][:], in_=ps_a[:, 1024 * ss:1024 * (ss + 1)],
                    func=AF.Exp),
                    waits=[("pe", ms), ("ve", m_e_use[sset])])
                m_exp_set[sset] = mex
                g0 = 2049 - 128 * kj0 + 512 * qc
                win = bass.AP(vexp[par], g0, [[VW, 128], [-128, 2], [1, 512]])
                ein = bass.AP(e_sb[sset], 0, [[1024, 128], [512, 2], [1, 512]])
                eout = bass.AP(e2_sb[sset], 0, [[1024, 128], [512, 2], [1, 512]])
                me2 = ve(lambda E, eo=eout, ei=ein, w=win: E.tensor_tensor(
                    eo, ei, w, ALU.mult),
                    waits=[("se", mex), ("gp", m_vmask[par]),
                           ("pe", m_e2_use[sset])])
                m_e_use[sset] = me2
                m_vcons[par] = me2
                m_e2g[g] = me2

            def emit_op(g):
                nonlocal mo
                sset = g % 2
                kj0 = 2 * g
                for t_ in range(2):
                    kj = kj0 + t_
                    wv = [("ve", m_e2g[g])]
                    if kj == 0:
                        wv.append(("ve", m_op_evac[orot]))
                    mo = mm(op_ps[orot],
                            bass.AP(vaug, 1040 * kj + 65 * h,
                                    [[NTI * 1040, 128], [1, 65]]),
                            e2_sb[sset][:, 512 * t_:512 * (t_ + 1)],
                            start=(kj == 0), stop=(kj == kmax - 1),
                            waits=wv, inc=(t_ == 1))
                m_e2_use[sset] = mo

            emit_scores(0)
            if qc < NQC - 1:
                ensure_stage(h, qc + 1)
            else:
                ensure_stage(h + 1, 0)
            front = norm_front()
            if ngrp > 1:
                emit_scores(1)
            emit_op(0)
            for g in range(2, ngrp):
                emit_scores(g)
                emit_op(g - 1)
            norm_back(front)
            if ngrp > 1:
                emit_op(ngrp - 1)
            pending_norm[0] = (h, qc, orot, mo)
            m_slot_pe[slot_u] = unit_last_scores[0]
    fr = norm_front()
    norm_back(fr)

    se_p3_done = cnt["se"]
    probe(16, q_bf[:, 0:4])
    dbg_store("o", q_bf[:], BF16)
    dbg_store("vexp15", vexp[1][:], BF16)

    for i in range(2):
        dma("gp", proj_sb[:, 4096 * i:4096 * (i + 1)],
            d_wproj.ap()[:, 4096 * i:4096 * (i + 1)], "sproj")
    m_proj_w = ("sproj", cnt["sproj"])
    psr_free = [None, None, None, None]
    m_y = None
    for qc in range(NQC):
        for dd in range(NCH):
            r = (qc * NCH + dd) % 4
            for din in range(NCH):
                w0 = []
                if din == 0:
                    w0 = [m_proj_w, ("se", se_p3_done),
                          ("ve", m_outv[qc])]
                    w0 += [(s_, v_) for s_, v_ in m_outd[qc].items()]
                    if psr_free[r]:
                        w0.append(psr_free[r])
                mmil = mm(psr[r],
                          proj_sb[:, 1024 * din + 128 * dd:1024 * din + 128 * (dd + 1)],
                          q_bf[:, 2048 * din + 512 * qc:2048 * din + 512 * (qc + 1)],
                          start=(din == 0), stop=(din == NCH - 1),
                          waits=w0, inc=(din == NCH - 1))
            m_y = ve(lambda E, d=dd, q=qc, r_=r: E.scalar_tensor_tensor(
                out=y_sb[:, 2048 * d + 512 * q:2048 * d + 512 * (q + 1)],
                in0=psr[r_], scalar=pbias[:, d:d + 1],
                in1=xln1[:, 2048 * d + 512 * q:2048 * d + 512 * (q + 1)],
                op0=ALU.add, op1=ALU.add),
                waits=[("pe", mmil), ld_pb])
            psr_free[r] = ("ve", m_y)

    barrier()
    probe(20, y_sb[:, 0:4])
    dbg_store("y", y_sb[:], F32)

    m_rows2 = ln_stats(y_sb, [("ve", m_y)], lnB)
    mbs = [ln_bcast(qc, m_rows2, lnB) for qc in range(NQC)]
    tmp = lnB["tmp"]
    ld_xc = [None, None]
    m_xc_cons = [0, 0]
    for d_ in range(2):
        ld_xc[d_] = ("sxc%d" % d_,
                     dma("sy" if d_ == 0 else "se", xchk[d_][:],
                         d_xT.ap()[128 * d_:128 * (d_ + 1), :], "sxc%d" % d_))
    m_x2 = None
    m_tmp_free5 = [0, 0]
    for d_ in range(NCH):
        c = d_ % 2
        for qc in range(NQC):
            t = qc % 2
            ma = ve(lambda E, d=d_, q=qc, tt=t: E.tensor_mul(
                tmp[tt][:], y_sb[:, 2048 * d + 512 * q:2048 * d + 512 * (q + 1)],
                ps_a[:, 512 * q:512 * (q + 1)]),
                waits=[("pe", mbs[qc]), ("ve", m_tmp_free5[t])])
            mxb = ve(lambda E, d=d_, q=qc, tt=t: E.tensor_scalar_add(
                xb[tt][:], xchk[d % 2][:, 512 * q:512 * (q + 1)],
                lnb[:, d:d + 1]),
                waits=[ld_xc[c], ld_lnb, ("ve", ma)])
            mt2 = ve(lambda E, tt=t, q=qc: E.tensor_sub(
                tmp[tt][:], tmp[tt][:], ps_b[:, 512 * q:512 * (q + 1)]),
                waits=[("ve", mxb)])
            m_x2 = ve(lambda E, d=d_, q=qc, tt=t: E.scalar_tensor_tensor(
                out=x2_sb[:, 2048 * d + 512 * q:2048 * d + 512 * (q + 1)],
                in0=tmp[tt][:], scalar=lng[:, d:d + 1], in1=xb[tt][:],
                op0=ALU.mult, op1=ALU.add),
                waits=[ld_lng, ("ve", mt2)])
            m_tmp_free5[t] = m_x2
        m_xc_cons[c] = m_x2
        if d_ + 2 < NCH:
            ld_xc[c] = ("sxc%d" % c,
                        dma("sy" if c == 0 else "se", xchk[c][:],
                            d_xT.ap()[128 * (d_ + 2):128 * (d_ + 3), :],
                            "sxc%d" % c, waits=[("ve", m_xc_cons[c])]))

    barrier()
    probe(24, x2_sb[:, 0:4])
    probe(52, lnB["rstd"][0:1, 0:4])
    dbg_store("x2", x2_sb[:], F32)

    m_rows3 = ln_stats(x2_sb, [("ve", m_x2)], lnB)
    m_xln2 = None
    m_tmp_free6 = [0, 0]
    for qc in range(NQC):
        mb = ln_bcast(qc, m_rows3, lnB)
        for d_ in range(NCH):
            t = d_ % 2
            ma = ve(lambda E, d=d_, q=qc, tt=t: E.tensor_mul(
                tmp[tt][:], x2_sb[:, 2048 * d + 512 * q:2048 * d + 512 * (q + 1)],
                ps_a[:, 512 * q:512 * (q + 1)]),
                waits=[("pe", mb), ("ve", m_tmp_free6[t])])
            m_xln2 = ve(lambda E, d=d_, q=qc, tt=t: E.tensor_sub(
                xln2[:, 2048 * d + 512 * q:2048 * d + 512 * (q + 1)],
                tmp[tt][:], ps_b[:, 512 * q:512 * (q + 1)]),
                waits=[("ve", ma)])
            m_tmp_free6[t] = m_xln2

    barrier()
    probe(28, xln2[:, 0:4])
    dbg_store("xln2", xln2[:], BF16)

    tr_ps = [ps_b[:, 0:128], ps_b[:, 512:640]]
    m_tr_free = [0, 0]
    m_tr = None
    for ti in range(NTI):
        for d_ in range(NCH):
            r = (ti * NCH + d_) % 2
            mt = emit("pe", lambda E, d=d_, t=ti, r_=r: E.transpose(
                tr_ps[r_], x2_sb[:, 2048 * d + 128 * t:2048 * d + 128 * (t + 1)],
                ident[:]),
                waits=[("gp", m_ident), ("ve", m_tr_free[r]),
                       ("ve", m_xln2)], inc="pe")
            m_tr = ve(lambda E, d=d_, t=ti, r_=r: E.tensor_copy(
                acc[:, 1024 * t + 128 * d:1024 * t + 128 * (d + 1)], tr_ps[r_]),
                waits=[("pe", mt)])
            m_tr_free[r] = m_tr

    barrier()
    probe(32, acc[:, 0:4])
    dbg_store("accinit", acc[:], F32)

    lg_ps = [ps_b[0:128, 1024:1536], ps_b[0:128, 1536:2048]]
    m_lg_free = [0, 0]
    m_comb = None
    for ti in range(NTI):
        r = ti % 2
        for din in range(NCH):
            mmil = mm(lg_ps[r][:, 0:7],
                      xln2[:, 2048 * din + 128 * ti:2048 * din + 128 * (ti + 1)],
                      rw_bf[:, 7 * din:7 * (din + 1)],
                      start=(din == 0), stop=(din == NCH - 1),
                      waits=([ld_rw, ("se", m_lg_free[r]), ("ve", m_xln2)]
                             if din == 0 else ()),
                      inc=(din == NCH - 1))
        mneg = ve(lambda E, r_=r: E.reduce_max(r_m1[:], lg_ps[r_][:, 0:7],
                                               axis=AX.X, negate=True),
                  waits=[("pe", mmil), ("ve", m_comb)])
        mer = se(lambda E, r_=r: E.activation(out=r_er[:], in_=lg_ps[r_][:, 0:7],
                                              func=AF.Exp, bias=r_m1[:]),
                 waits=[("ve", mneg)])
        m_lg_free[r] = mer
        ms = ve(lambda E: E.reduce_sum(r_s1[:], r_er[:], axis=AX.X),
                waits=[("se", mer)])
        mrc = ve(lambda E: E.reciprocal(r_s1[:], r_s1[:]), waits=[("ve", ms)])
        mp = ve(lambda E: E.tensor_scalar_mul(r_er[:], r_er[:], r_s1[:]),
                waits=[("ve", mrc)])
        mm1 = ve(lambda E: E.reduce_max(r_m1[:], r_er[:], axis=AX.X),
                 waits=[("ve", mp)])
        meq = ve(lambda E: E.tensor_single_scalar(r_t1[:], r_er[:], r_m1[:],
                                                  ALU.is_ge),
                 waits=[("ve", mm1)])
        mp2 = ve(lambda E: E.scalar_tensor_tensor(
            out=r_t2[:], in0=r_t1[:], scalar=-1e9, in1=r_er[:],
            op0=ALU.mult, op1=ALU.add), waits=[("ve", meq)])
        mm2 = ve(lambda E: E.reduce_max(r_m2[:], r_t2[:], axis=AX.X),
                 waits=[("ve", mp2)])
        mkp = ve(lambda E: E.tensor_single_scalar(r_t1[:], r_er[:], r_m2[:],
                                                  ALU.is_ge),
                 waits=[("ve", mm2)])
        mcb = ve(lambda E: E.tensor_mul(r_t2[:], r_er[:], r_t1[:]),
                 waits=[("ve", mkp)])
        mdn = ve(lambda E: E.tensor_add(r_m1[:], r_m1[:], r_m2[:]),
                 waits=[("ve", mcb)])
        mdn2 = ve(lambda E: E.tensor_scalar_add(r_m1[:], r_m1[:], 1e-9),
                  waits=[("ve", mdn)])
        mrc2 = ve(lambda E: E.reciprocal(r_m1[:], r_m1[:]), waits=[("ve", mdn2)])
        m_comb = ve(lambda E, t=ti: E.tensor_scalar_mul(
            combine[:, 7 * t:7 * (t + 1)], r_t2[:], r_m1[:]),
            waits=[("ve", mrc2)])

    probe(36, combine[:, 0:4])
    dbg_store("combine", combine[:], F32)

    g_ps = [ps_a[:, 0:512], ps_a[:, 512:1024]]
    u_ps = [ps_a[:, 1024:1536], ps_a[:, 1536:2048]]
    da_ps = [ps_a[:, 0:256], ps_a[:, 512:768], ps_a[:, 1024:1280],
             ps_a[:, 1536:1792], ps_b[:, 0:256], ps_b[:, 512:768],
             ps_b[:, 1024:1280], ps_b[:, 1536:1792]]
    m_gfree = [0, 0]
    m_ufree = [0, 0]
    m_gslot = [0, 0]
    m_uslot = [0, 0]
    m_dslot = [0, 0]
    m_dafree = [0] * 8
    m_gsil_free = [0, 0]
    m_lastdown = 0
    grot = 0

    for th in range(2):
        for e in range(NEXP):
            mh = None
            for s_ in range(NFF):
                sl = s_ % 2
                mg_ld = ("sg%d" % sl,
                         dma("gp", gring[sl][:], d_g.ap()[e, s_, :, :],
                             "sg%d" % sl,
                             waits=[("pe", m_gslot[sl])]))
                mu_ld = ("su%d" % sl,
                         dma("gp", uring[sl][:], d_u.ap()[e, s_, :, :],
                             "su%d" % sl,
                             waits=[("pe", m_uslot[sl])]))
                for qc2 in range(2):
                    pr = grot
                    grot = 1 - grot
                    for din in range(NCH):
                        mg = mm(g_ps[pr],
                                gring[sl][:, 128 * din:128 * (din + 1)],
                                xln2[:, 2048 * din + 1024 * th + 512 * qc2:
                                     2048 * din + 1024 * th + 512 * (qc2 + 1)],
                                start=(din == 0), stop=(din == NCH - 1),
                                waits=([mg_ld, ("se", m_gfree[pr]),
                                        ("ve", max(m_dafree)),
                                        ("ve", m_xln2)]
                                       if din == 0 else ()),
                                inc=(din == NCH - 1))
                    for din in range(NCH):
                        mu = mm(u_ps[pr],
                                uring[sl][:, 128 * din:128 * (din + 1)],
                                xln2[:, 2048 * din + 1024 * th + 512 * qc2:
                                     2048 * din + 1024 * th + 512 * (qc2 + 1)],
                                start=(din == 0), stop=(din == NCH - 1),
                                waits=([mu_ld, ("ve", m_ufree[pr]),
                                        ("ve", max(m_dafree)),
                                        ("ve", m_xln2)]
                                       if din == 0 else ()),
                                inc=(din == NCH - 1))
                    if qc2 == 1:
                        m_gslot[sl] = mg
                        m_uslot[sl] = mu
                    msil = se(lambda E, p=pr: E.activation(
                        out=gsil[p][:], in_=g_ps[p], func=AF.Silu),
                        waits=[("pe", mg), ("ve", m_gsil_free[pr])])
                    m_gfree[pr] = msil
                    mh = ve(lambda E, p=pr, DD=s_, q=qc2: E.tensor_mul(
                        h_sb[:, 1024 * DD + 512 * q:1024 * DD + 512 * (q + 1)],
                        gsil[p][:], u_ps[p]),
                        waits=[("se", msil), ("pe", mu), ("pe", m_lastdown)])
                    m_ufree[pr] = mh
                    m_gsil_free[pr] = mh
            for ds in range(4):
                for hb in range(2):
                    sl = (2 * ds + hb) % 2
                    md_ld = ("sd%d" % sl,
                             dma("gp", dring[sl][:],
                                 d_dn.ap()[e, 2 * ds + hb, :, :],
                                 "sd%d" % sl,
                                 waits=[("pe", m_dslot[sl])]))
                    for tl in range(8):
                        ti = 8 * th + tl
                        for Dp in range(16):
                            Dc = 16 * hb + Dp
                            w0 = []
                            if Dp == 0:
                                w0 = [md_ld, ("ve", mh), ("ve", m_tr),
                                      ("se", m_lg_free[0]), ("se", m_lg_free[1]),
                                      ("ve", m_comb)]
                                if hb == 0:
                                    w0.append(("ve", m_dafree[tl]))
                            md = mm(da_ps[tl],
                                    h_sb[:, 1024 * Dc + 128 * tl:1024 * Dc + 128 * (tl + 1)],
                                    dring[sl][:, 256 * Dp:256 * (Dp + 1)],
                                    start=(Dc == 0), stop=(Dc == NFF - 1),
                                    waits=w0, inc=(Dp == 15))
                        if hb == 1:
                            ti_ = 8 * th + tl
                            cb_ap = (cbones[:, 0:1] if e == 0
                                     else combine[:, 7 * ti_ + (e - 1):7 * ti_ + e])
                            m_dafree[tl] = ve(
                                lambda E, p=tl, t=ti_, d2=ds, c=cb_ap: E.scalar_tensor_tensor(
                                    out=acc[:, 1024 * t + 256 * d2:1024 * t + 256 * (d2 + 1)],
                                    in0=da_ps[p], scalar=c,
                                    in1=acc[:, 1024 * t + 256 * d2:1024 * t + 256 * (d2 + 1)],
                                    op0=ALU.mult, op1=ALU.add),
                                waits=[("pe", md), ("ve", m_comb), ("ve", m_tr)])
                    m_dslot[sl] = md
            m_lastdown = md
            if th == 0 and e == 0:
                dbg_store("h_e0", h_sb[:], BF16)
                dbg_store("acc_e0", acc[:], F32)
        if th == 0:
            m_f0 = ve_fence()
            for ti in range(8):
                store(d_out.ap()[128 * ti:128 * (ti + 1), :],
                      acc[:, 1024 * ti:1024 * (ti + 1)], waits=[("ve", m_f0)])

    barrier()

    probe(40, h_sb[:, 0:4])
    probe(44, acc[:, 4:8])
    if _PROBE:
        d_probe = nc.dram_tensor("probe_out", [128, 64], F32, kind="ExternalOutput")
    m_out_fence = ve_fence()
    if _PROBE:
        store(d_probe.ap(), probe_sb[:], waits=[("ve", m_out_fence)])
    for ti in range(8, NTI):
        store(d_out.ap()[128 * ti:128 * (ti + 1), :],
              acc[:, 1024 * ti:1024 * (ti + 1)], waits=[("ve", m_out_fence)])
    ops.append(("sy", None, (("st", cnt["st"]),), None, 0))

    eng_ops = {e: [] for e in ENGS}
    for (eng, fn, ws, incsem, incn) in ops:
        eng_ops[eng].append((fn, ws, incsem, incn))

    def run_engine(E, elist):
        for (fn, ws, incsem, incn) in elist:
            for (s, v) in ws:
                E.wait_ge(sems[s], v)
            if fn is not None:
                inst = fn(E)
                if incsem is not None:
                    inst.then_inc(sems[incsem], incn)

    with nc.Block() as block:
        @block.tensor
        def _(eng):
            run_engine(eng, eng_ops["pe"])

        @block.vector
        def _(eng):
            run_engine(eng, eng_ops["ve"])

        @block.scalar
        def _(eng):
            run_engine(eng, eng_ops["se"])

        @block.gpsimd
        def _(eng):
            run_engine(eng, eng_ops["gp"])

        @block.sync
        def _(eng):
            run_engine(eng, eng_ops["sy"])

    nc._dbg_names = dbg_names
    return nc


_NC_CACHE = None


def _get_program():
    global _NC_CACHE
    if _NC_CACHE is None:
        _NC_CACHE = build_program()
    return _NC_CACHE


def kernel(L0_attn_x, L0_attn_W_w, L0_attn_W_b, L0_attn_proj_w, L0_attn_proj_b,
           L0_attn_ln_g, L0_attn_ln_b, L0_attn_rpb, L0_ffn_shared_up,
           L0_ffn_shared_gate, L0_ffn_shared_down, L0_ffn_routed_up,
           L0_ffn_routed_gate, L0_ffn_routed_down, L0_ffn_router_w):
    x = np.asarray(L0_attn_x, dtype=np.float32)
    W_w = np.asarray(L0_attn_W_w, dtype=np.float32)
    W_b = np.asarray(L0_attn_W_b, dtype=np.float32)
    proj_w = np.asarray(L0_attn_proj_w, dtype=np.float32)
    proj_b = np.asarray(L0_attn_proj_b, dtype=np.float32)
    ln_g = np.asarray(L0_attn_ln_g, dtype=np.float32)
    ln_b = np.asarray(L0_attn_ln_b, dtype=np.float32)
    rpb = np.asarray(L0_attn_rpb, dtype=np.float32)
    s_up = np.asarray(L0_ffn_shared_up, dtype=np.float32)
    s_gate = np.asarray(L0_ffn_shared_gate, dtype=np.float32)
    s_down = np.asarray(L0_ffn_shared_down, dtype=np.float32)
    r_up = np.asarray(L0_ffn_routed_up, dtype=np.float32)
    r_gate = np.asarray(L0_ffn_routed_gate, dtype=np.float32)
    r_down = np.asarray(L0_ffn_routed_down, dtype=np.float32)
    router_w = np.asarray(L0_ffn_router_w, dtype=np.float32)

    wqkvT = np.ascontiguousarray(W_w.T)
    wqkv = np.ascontiguousarray(
        wqkvT.reshape(NCH, 128, 3072).transpose(1, 0, 2).reshape(128, NCH * 3072))
    projT = np.ascontiguousarray(proj_w.T)
    wproj = np.ascontiguousarray(
        projT.reshape(NCH, 128, 1024).transpose(1, 0, 2).reshape(128, NCH * 1024))
    wbqk_h = np.ascontiguousarray(W_b[:2048].reshape(16, 128).T)
    bv_h = np.ascontiguousarray(W_b[2048:].reshape(1, 1024))
    pb_h = np.ascontiguousarray(proj_b.reshape(8, 128).T)
    lng_h = np.ascontiguousarray(ln_g.reshape(8, 128).T)
    lnb_h = np.ascontiguousarray(ln_b.reshape(8, 128).T)
    rpbpad = np.zeros((H, PADLEN), dtype=np.float32)
    rpbpad[:, 127:127 + 4095] = rpb.T
    rpbflip = rpbpad[:, ::-1]
    sw = np.lib.stride_tricks.sliding_window_view(rpbflip, VW, axis=1)
    rpbexp = np.ascontiguousarray(sw[:, 129:257][:, ::-1, :])
    rwT = np.ascontiguousarray(router_w.T)
    rw_h = np.ascontiguousarray(
        rwT.reshape(NCH, 128, 7).transpose(1, 0, 2).reshape(128, NCH * 7))

    gates = np.concatenate([s_gate[None], r_gate], axis=0)
    ups = np.concatenate([s_up[None], r_up], axis=0)
    downs = np.concatenate([s_down[None], r_down], axis=0)

    def pack_gu(w):
        t = w.reshape(NEXP, NFF, 128, NCH, 128)
        t = t.transpose(0, 1, 4, 3, 2)
        return np.ascontiguousarray(t.reshape(NEXP, NFF, 128, NCH * 128))

    def pack_down(w):
        t = w.reshape(NEXP, 4, 256, 2, 16, 128)
        t = t.transpose(0, 1, 3, 5, 4, 2)
        return np.ascontiguousarray(t.reshape(NEXP, 8, 128, 16 * 256))

    g_all = pack_gu(gates)
    u_all = pack_gu(ups)
    dn_all = pack_down(downs)

    nc = _get_program()
    in_maps = []
    for c in range(8):
        in_maps.append({
            "xT": np.ascontiguousarray(x[c].T),
            "wqkv": wqkv, "wproj": wproj, "wbqk": wbqk_h, "bv": bv_h,
            "pbias": pb_h, "lng": lng_h, "lnb": lnb_h, "rpbpad": rpbexp,
            "rw": rw_h, "g_all": g_all, "u_all": u_all, "dn_all": dn_all,
        })
    res = run_bass_kernel_spmd(nc, in_maps, core_ids=list(range(8)))
    return np.stack([res.results[c]["out"] for c in range(8)], axis=0)
